# version 13
# baseline (speedup 1.0000x reference)
import contextlib
import math
import numpy as np
import ml_dtypes
import concourse.bass as bass
import concourse.mybir as mybir
from concourse.bass_utils import run_bass_kernel_spmd

F32 = mybir.dt.float32
BF16 = mybir.dt.bfloat16
AF = mybir.ActivationFunctionType
OP = mybir.AluOpType

B, S, D = 4, 8192, 1024
T = 4096
NCH = D // 128
TT = 512
NT = T // TT
HEADS, DH, WIN, INNER, HID = 8, 64, 512, 512, 2730
YST = T + 1
LN05 = math.log(0.5)
RG = [[0, 1], [2, 3], [4, 5], [6, 7]]

BF = ml_dtypes.bfloat16


def host_constants(core_id):
    h = core_id % 2
    t0g = h * T
    c = {}
    c["ident"] = np.eye(128, dtype=BF)
    c["ones128b"] = np.ones((128, 1), dtype=BF)
    P = np.zeros((128, 128), dtype=np.float32)
    for hh in range(2):
        o = hh * 64
        for d in range(32):
            P[o + d, o + d + 32] = -1.0
            P[o + d + 32, o + d] = 1.0
    c["PT"] = P.T.astype(BF)
    c["PTf"] = P.T.astype(np.float32)
    half_idx = np.arange(0, DH, 2, dtype=np.float64)
    inv_freq = 1.0 / 10000.0 ** (half_idx / DH)
    t = np.arange(t0g, t0g + T, dtype=np.float64)
    freqs = t[:, None] * inv_freq[None, :]
    freqs = np.concatenate([freqs, freqs], axis=-1)
    base = (half_idx + 0.4 * DH) / (1.4 * DH)
    power = (t - S // 2) / float(WIN // 2)
    scale = base[None, :] ** power[:, None]
    scale = np.concatenate([scale, scale], axis=-1)
    cos, sin = np.cos(freqs), np.sin(freqs)
    dup = lambda a: np.concatenate([a, a], axis=0).astype(np.float32)
    c["csq"] = dup((cos * scale * 8.0).T)
    c["ssq"] = dup((sin * scale * 8.0).T)
    c["csk"] = dup((cos / scale).T)
    c["ssk"] = dup((sin / scale).T)
    ii = np.arange(128)[None, :]
    pp = np.arange(128)[:, None]
    masks = np.zeros((5, 5, 128, 128), dtype=BF)
    for cqc in range(5):
        for cr in range(5):
            off = 128 * (cr - 4)
            valid = (off + pp - ii >= -512) & (off + pp - ii <= 0)
            if cqc < 4 and h == 0 and (cqc - 4 + cr) < 0:
                valid = np.zeros_like(valid)
            masks[cqc, cr] = valid.astype(BF)
    c["masks"] = masks
    c["wgate"] = np.full((128, 1), 1.0 if h == 0 else 0.0, dtype=np.float32)
    c["rgate"] = np.full((128, 1), 0.0 if h == 0 else 1.0, dtype=np.float32)
    ind = np.zeros((4, 128, 8), dtype=np.float32)
    for cc in range(4):
        ind[cc, 0:64, 2 * cc] = 1.0
        ind[cc, 64:128, 2 * cc + 1] = 1.0
    c["ind8T"] = ind
    ind2 = np.zeros((4, 8, 128), dtype=np.float32)
    for cc in range(4):
        ind2[cc, 2 * cc, 0:64] = 1.0
        ind2[cc, 2 * cc + 1, 64:128] = 1.0
    c["ind8"] = ind2
    return c


def make_in_map(inputs, core_id, consts):
    b, h = core_id // 2, core_id % 2
    x = np.asarray(inputs["L0_attn_x"])[b, h * T:(h + 1) * T, :]
    m = {
        "x_fm": np.ascontiguousarray(x.T.astype(np.float32)),
        "attn_W": np.asarray(inputs["L0_attn_W"], dtype=np.float32),
        "attn_b": np.asarray(inputs["L0_attn_b"], dtype=np.float32).reshape(1, 3 * D),
        "wqkv": np.asarray(inputs["L0_ffn_wqkv"], dtype=np.float32),
        "qscale": np.tile(np.asarray(inputs["L0_ffn_q_scale"], dtype=np.float32), (1, 2)),
        "kscale": np.tile(np.asarray(inputs["L0_ffn_k_scale"], dtype=np.float32), (1, 2)),
        "wo": np.asarray(inputs["L0_ffn_wo"], dtype=np.float32),
        "ln_g": np.asarray(inputs["L0_ffn_ln_g"], dtype=np.float32),
        "ln_b": np.asarray(inputs["L0_ffn_ln_b"], dtype=np.float32),
        "w1": np.asarray(inputs["L0_ffn_w1"], dtype=np.float32),
        "b1": np.asarray(inputs["L0_ffn_b1"], dtype=np.float32),
        "w2": np.asarray(inputs["L0_ffn_w2"], dtype=np.float32),
        "b2": np.asarray(inputs["L0_ffn_b2"], dtype=np.float32),
    }
    m.update(consts[core_id])
    return m


class Prog:

    def __init__(self, nc, ctx):
        self.nc = nc
        self.ctx = ctx
        self.sems = {}
        self.cnt = {}
        self.last_wait = {}
        for n in ("pe", "dv", "ac", "po", "cc"):
            self.mksem(n)

    def mksem(self, n):
        if n not in self.sems:
            self.sems[n] = self.ctx.enter_context(self.nc.semaphore(n))
            self.cnt[n] = 0
        return self.sems[n]

    def inc(self, instr, name, amt=1):
        instr.then_inc(self.sems[name], amt)
        self.cnt[name] += amt
        return self.cnt[name]

    def dma(self, eng, tag, out, in_, **kw):
        self.mksem(tag)
        i = eng.dma_start(out=out, in_=in_, **kw)
        return tag, self.inc(i, tag, 16)

    def gdma(self, tag, out, in_, **kw):
        return self.dma(self.nc.gpsimd, tag, out, in_, **kw)

    def waitt(self, eng, tok):
        self.wait(eng, tok[0], tok[1])

    def wait(self, eng, name, val=None):
        v = self.cnt[name] if val is None else val
        if v <= 0:
            return
        key = (eng.engine, name)
        if self.last_wait.get(key, -1) >= v:
            return
        eng.wait_ge(self.sems[name], v)
        self.last_wait[key] = v


class MK:

    def __init__(self, debug_taps=()):
        self.taps = set(debug_taps)
        self.upto = 99
        self.nc = bass.Bass("TRN2", target_bir_lowering=False,
                            dynamic_dma_scratch_size=16384)
        self.extra_outputs = []
        self._uid = 0

    def uname(self, n):
        self._uid += 1
        return f"{n}_{self._uid}"

    def declare(self):
        nc = self.nc
        specs = [
            ("x_fm", [D, T], F32), ("attn_W", [3 * D, D], F32), ("attn_b", [1, 3 * D], F32),
            ("wqkv", [2, D, 3 * INNER], F32), ("qscale", [2, 128], F32), ("kscale", [2, 128], F32),
            ("wo", [2, INNER, D], F32), ("ln_g", [2, D], F32), ("ln_b", [2, D], F32),
            ("w1", [2, D, 2 * HID], F32), ("b1", [2, 2 * HID], F32),
            ("w2", [2, HID, D], F32), ("b2", [2, D], F32),
            ("ident", [128, 128], BF16), ("ones128b", [128, 1], BF16),
            ("PT", [128, 128], BF16), ("PTf", [128, 128], F32),
            ("csq", [128, T], F32), ("ssq", [128, T], F32),
            ("csk", [128, T], F32), ("ssk", [128, T], F32),
            ("masks", [5, 5, 128, 128], BF16),
            ("wgate", [128, 1], F32), ("rgate", [128, 1], F32),
            ("ind8T", [4, 128, 8], F32), ("ind8", [4, 8, 128], F32),
        ]
        self.inp = {n: nc.dram_tensor(n, sh, dt, kind="ExternalInput").ap() for n, sh, dt in specs}
        self.out_fm = nc.dram_tensor("out_fm", [D, T], F32, kind="ExternalOutput").ap()
        self.x2 = nc.dram_tensor("x2_hbm", [D, T], F32).ap()
        self.x3 = nc.dram_tensor("x3_hbm", [D, T], F32).ap()
        self.h2d = nc.dram_tensor("h2_hbm", [D, T], BF16).ap()
        self.ycol_in = nc.dram_tensor("ycol_in", [128, NCH], F32).ap()
        self.ycol_out = nc.dram_tensor("ycol_out", [128, NCH], F32).ap()
        self.qd = nc.dram_tensor("q_hbm", [4, 128, T], BF16).ap()
        self.kd = nc.dram_tensor("k_hbm", [32, 4, 128, 128], BF16).ap()
        self.vd = nc.dram_tensor("v_hbm", [32, 128, INNER], BF16).ap()
        self.khalo_in = nc.dram_tensor("khalo_in", [4, 4, 128, 128], BF16).ap()
        self.khalo = nc.dram_tensor("khalo", [4, 4, 128, 128], BF16).ap()
        self.vhalo_in = nc.dram_tensor("vhalo_in", [4, 128, INNER], BF16).ap()
        self.vhalo = nc.dram_tensor("vhalo", [4, 128, INNER], BF16).ap()
        self.zrowd = nc.dram_tensor("zrow_d", [8, 512], F32).ap()
        self.gbd = nc.dram_tensor("gb_d", [2, D], BF16).ap()
        self.h1d = nc.dram_tensor("h1_hbm", [D, T], BF16).ap()
        self.aod = nc.dram_tensor("ao_hbm", [D, T], F32).ap()
        self.kgin = nc.dram_tensor("kgin", [32, 8, 128, 128], BF16).ap()
        self.vgin = nc.dram_tensor("vgin", [32, 128, 1024], BF16).ap()
        self.kgf = nc.dram_tensor("kgf", [2, 32, 8, 128, 128], BF16).ap()
        self.vgf = nc.dram_tensor("vgf", [2, 32, 128, 1024], BF16).ap()
        self.kn2d = nc.dram_tensor("kn2d", [T], F32).ap()
        self.kn2f = nc.dram_tensor("kn2f", [2, T], F32).ap()
        self.qm2d = nc.dram_tensor("qm2d", [1, 1], F32).ap()
        self.qm2f = nc.dram_tensor("qm2f", [1, 1], F32).ap()
        self.ed = nc.dram_tensor("e_hbm", [64, 8, 128, 512], BF16).ap()
        self.zd = nc.dram_tensor("zd", [128, 64], F32).ap()
        self.ztd = nc.dram_tensor("ztd", [128, 64], F32).ap()
    def alloc(self, ctx):
        nc = self.nc
        ent = ctx.enter_context
        self.y = ent(nc.sbuf_tensor("y", [128, NCH * YST], F32))
        self.smalls = ent(nc.sbuf_tensor("smalls", [128, 2112], F32))
        self.colsc = self.smalls[:, 0:64]
        self.row = self.smalls[0:1, 64:64 + 4 * TT]
        self.cpool = ent(nc.sbuf_tensor("cpool", [128, 8], F32))
        self.cpool_vals = {}
        self.cbf = ent(nc.sbuf_tensor("cbf", [128, 512], BF16))
        self.onesrow = ent(nc.sbuf_tensor("onesrow", [1, 128], F32))
        self.onesb = ent(nc.sbuf_tensor("onesb", [1, 512], BF16))
        self.ind8Ts = ent(nc.sbuf_tensor("ind8Ts", [128, 32], F32))
        self.ind8s = ent(nc.sbuf_tensor("ind8s", [8, 512], F32))
        self.PTf = ent(nc.sbuf_tensor("PTf_sb", [128, 128], F32))
        self.ps = [ent(nc.psum_tensor(f"ps{i}", [128, 1024], F32)) for i in range(4)]
        self.p = Prog(nc, ctx)
        self.negC = self.smalls[:, 2:3]

    def bank(self, i):
        return self.ps[i // 2][:, (i % 2) * 512:(i % 2) * 512 + 512]

    def load_consts(self):
        nc, p = self.nc, self.p
        Y = nc.sync
        p.dma(Y, "q_const", self.cbf[:, 0:128], self.inp["ident"])
        p.dma(Y, "q_const", self.cbf[:, 128:256], self.inp["PT"])
        p.dma(Y, "q_const", self.cbf[:, 256:257], self.inp["ones128b"])
        p.dma(Y, "q_const", self.smalls[:, 0:1], self.inp["wgate"])
        p.dma(Y, "q_const", self.smalls[:, 1:2], self.inp["rgate"])
        p.dma(Y, "q_const", self.PTf[:, :], self.inp["PTf"])
        p.dma(Y, "q_const", self.ind8Ts[:, :].rearrange("p (a b) -> p a b", a=4),
              self.inp["ind8T"].rearrange("a p b -> p a b"))
        p.dma(Y, "q_const", self.ind8s[:, :].rearrange("p (a b) -> p a b", a=4),
              self.inp["ind8"].rearrange("a p b -> p a b"))
        self.ident = self.cbf[:, 0:128]
        self.PT = self.cbf[:, 128:256]
        self.ones128b = self.cbf[:, 256:257]
        self.wgate = self.smalls[:, 0:1]
        self.rgate = self.smalls[:, 1:2]
        self.consts_ld = ("q_const", p.cnt["q_const"])
        p.inc(nc.vector.memset(self.onesrow[:, :], 1.0), "dv")
        p.inc(nc.vector.memset(self.onesb[:, :], 1.0), "dv")
        for v in (0.0, 1e-5, LN05, 1e-18, -math.log(1024.0)):
            self.constcol(v)
        self.cpool_done = p.cnt["dv"]

    def constcol(self, value):
        if value not in self.cpool_vals:
            i = len(self.cpool_vals)
            ap = self.cpool[:, i:i + 1]
            self.p.inc(self.nc.vector.memset(ap, float(value)), "dv")
            self.nc.const_aps.register(F32, float(value), ap)
            self.cpool_vals[value] = ap
        return self.cpool_vals[value]

    def chunks(self, buf, n=TT):
        return [buf[:, c * n:(c + 1) * n] for c in range(NCH)]

    def ytile(self, c, c0, n=TT, shift=0):
        return self.y[:, c * YST + 1 + c0 - shift: c * YST + 1 + c0 - shift + n]

    def fm(self, dram, c0, n=TT):
        return dram.rearrange("(c p) t -> p c t", p=128)[:, :, c0:c0 + n]

    def fence(self, engines=None, sems=None):
        nc, p = self.nc, self.p
        engines = engines or [nc.tensor, nc.vector, nc.scalar, nc.gpsimd, nc.sync]
        for eng in engines:
            for nm, c in p.cnt.items():
                if c and (sems is None or nm in sems):
                    p.wait(eng, nm, c)

    def emit_boundary(self, src, add_src, write_x, make):
        nc, p = self.nc, self.p
        TEN, V, A, G, Y = nc.tensor, nc.vector, nc.scalar, nc.gpsimd, nc.sync
        with contextlib.ExitStack() as ph:
            ent = ph.enter_context
            xt = [ent(nc.sbuf_tensor(self.uname(f"bxt{i}"), [128, NCH * TT], F32)) for i in range(2)]
            xb = [ent(nc.sbuf_tensor(self.uname(f"bxb{i}"), [128, NCH * TT], BF16)) for i in range(2)]
            at = None
            sqb = ent(nc.sbuf_tensor(self.uname("bsq"), [128, NCH * TT], BF16))
            s1s = [self.bank(0)[0:1, :], self.bank(4)[0:1, :]]
            s2s = [self.bank(1)[0:1, :], self.bank(5)[0:1, :]]
            rbs = [self.bank(2), self.bank(6)]
            sbs = [self.bank(3), self.bank(7)]
            bufdone = {0: 0, 1: 0}
            h2wd = {0: ("pe", 0), 1: ("pe", 0)}
            wxd = {0: ("pe", 0), 1: ("pe", 0)}
            p.wait(nc.scalar, "dv", self.cpool_done)
            pe_prev = {0: 0, 1: 0}
            dv_prev = {0: 0, 1: 0}
            for t in range(NT):
                bi = t % 2
                xtb, xbb = xt[bi], xb[bi]
                xtc, xbc, sqc = self.chunks(xtb), self.chunks(xbb), self.chunks(sqb)
                s1, s2, rb, sb = s1s[bi], s2s[bi], rbs[bi], sbs[bi]
                c0 = t * TT
                p.wait(Y, "dv", bufdone[bi])
                p.waitt(Y, wxd[bi])
                ld = p.dma(Y, f"q_xt{bi}", xtb[:, :].rearrange("p (c t) -> p c t", c=NCH),
                           self.fm(src, c0))
                wx = None
                if add_src is not None and add_src != "y":
                    p.waitt(G, ld)
                    lda = p.gdma(f"g_at{bi}", xtb[:, :].rearrange("p (c t) -> p c t", c=NCH),
                                 self.fm(add_src, c0), accum_op=OP.add)
                    p.waitt(V, lda)
                    p.waitt(Y, lda)
                    added = p.cnt["dv"]
                elif add_src == "y":
                    p.waitt(V, ld)
                    last = None
                    for c in range(NCH):
                        src2 = self.ytile(c, c0)
                        last = V.tensor_tensor(out=xtc[c], in0=xtc[c], in1=src2, op=OP.add)
                    added = p.inc(last, "dv")
                    p.wait(Y, "dv", added)
                if add_src is not None:
                    wx = p.dma(Y, f"q_xw{bi}", self.fm(write_x, c0),
                               xtb[:, :].rearrange("p (c t) -> p c t", c=NCH))
                    wxd[bi] = wx
                    if make is None:
                        bufdone[bi] = added
                        continue
                    p.wait(V, "dv", added)
                    p.waitt(V, h2wd[bi])
                    for c in range(NCH):
                        last = V.tensor_copy(out=xbc[c], in_=xtc[c])
                    bconv = p.inc(last, "dv")
                    p.wait(A, "dv", bconv)
                    p.wait(TEN, "dv", bconv)
                else:
                    p.wait(G, "dv", bufdone[bi])
                    gld = p.gdma(f"qg_xb{bi}", xbb[:, :].rearrange("p (c t) -> p c t", c=NCH),
                                 self.fm(src, c0))
                    p.waitt(A, gld)
                    p.waitt(TEN, gld)
                last = None
                for c in range(NCH):
                    last = A.activation(out=sqc[c], in_=xbc[c], func=AF.Square)
                acsq = p.inc(last, "ac")
                p.waitt(TEN, self.consts_ld)
                p.wait(TEN, "pe", pe_prev[bi])
                p.wait(TEN, "dv", dv_prev[bi])
                for c in range(NCH):
                    TEN.matmul(s1, self.ones128b, xbc[c], start=(c == 0), stop=(c == NCH - 1))
                p.wait(TEN, "ac", acsq)
                for c in range(NCH):
                    mm = TEN.matmul(s2, self.ones128b, sqc[c], start=(c == 0), stop=(c == NCH - 1))
                pe_stats = p.inc(mm, "pe")
                r = self.row
                mu, t1r, t2r = (r[:, i * TT:(i + 1) * TT] for i in range(3))
                p.wait(V, "pe", pe_stats)
                d1 = p.inc(V.tensor_scalar(out=mu, in0=s1, scalar1=1.0 / D, scalar2=None,
                                           op0=OP.mult), "dv")
                p.wait(V, "dv", d1)
                d2 = p.inc(V.tensor_tensor(out=t1r, in0=mu, in1=mu, op=OP.mult), "dv")
                p.wait(V, "dv", d2)
                d3 = p.inc(V.scalar_tensor_tensor(out=t1r, in0=s2, scalar=1.0 / D, in1=t1r,
                                                  op0=OP.mult, op1=OP.subtract), "dv")
                p.wait(A, "dv", d3)
                p.wait(A, "pe", pe_prev[1 - bi])
                a1 = p.inc(A.activation(out=t2r, in_=t1r, func=AF.Ln, bias=1e-5, scale=1.0), "ac")
                p.wait(A, "ac", a1)
                bias = LN05 if make == "y" else 0.0
                a2 = p.inc(A.activation(out=t2r, in_=t2r, func=AF.Exp, bias=bias, scale=-0.5), "ac")
                rp = t2r
                p.wait(V, "ac", a2)
                d4 = p.inc(V.tensor_tensor(out=t1r, in0=mu, in1=rp, op=OP.mult), "dv")
                sp = t1r
                p.wait(TEN, "ac", a2)
                p.wait(TEN, "dv", d4)
                TEN.matmul(rb, self.onesrow[:, :], rp, start=True, stop=True)
                mmb = TEN.matmul(sb, self.onesrow[:, :], sp, start=True, stop=True)
                pe_b = p.inc(mmb, "pe")
                pe_prev[bi] = pe_b
                p.wait(V, "pe", pe_b)
                p.waitt(V, ld)
                if wx is not None:
                    p.waitt(V, wx)
                for c in range(NCH):
                    last = V.tensor_tensor(out=xtc[c], in0=xtc[c], in1=rb, op=OP.mult)
                dm = p.inc(last, "dv")
                p.wait(V, "dv", dm)
                for c in range(NCH):
                    dst = self.ytile(c, c0) if make == "y" else xbc[c]
                    last = V.tensor_tensor(out=dst, in0=xtc[c], in1=sb, op=OP.subtract)
                vd = p.inc(last, "dv")
                dv_prev[bi] = vd
                if make == "h2":
                    p.wait(Y, "dv", vd)
                    h2wd[bi] = p.dma(Y, f"q_h2w{bi}", self.fm(self.h2d, c0),
                                     xbb[:, :].rearrange("p (c t) -> p c t", c=NCH))
                bufdone[bi] = vd
            self.fence()

    def emit_ycol_exchange(self):
        nc, p = self.nc, self.p
        V, G = nc.vector, nc.gpsimd
        yv = self.y[:, :].rearrange("p (c t) -> p c t", c=NCH)
        lastcols = yv[:, :, YST - 1]
        tmp = self.smalls[:, 8:8 + NCH]
        p.waitt(V, self.consts_ld)
        p.wait(V, "dv")
        d1 = p.inc(V.tensor_scalar(out=tmp, in0=lastcols, scalar1=self.wgate, scalar2=None,
                                   op0=OP.mult), "dv")
        p.wait(G, "dv", d1)
        g1 = p.gdma("qg_ycol", self.ycol_in, tmp)
        p.waitt(G, g1)
        cc = p.inc(G.collective_compute("AllReduce", OP.add, replica_groups=RG,
                                        ins=[self.ycol_in], outs=[self.ycol_out]), "cc")
        p.wait(G, "cc", cc)
        hcol = self.smalls[:, 8 + NCH:8 + 2 * NCH]
        g2 = p.gdma("qg_ycol", hcol, self.ycol_out)
        p.waitt(V, g2)
        d2 = p.inc(V.tensor_scalar(out=hcol, in0=hcol, scalar1=self.rgate, scalar2=None,
                                   op0=OP.mult), "dv")
        p.wait(V, "dv", d2)
        halocols = yv[:, :, 0]
        return p.inc(V.tensor_copy(out=halocols, in_=hcol), "dv")

    def emit_ffn_qkv(self, l):
        nc, p = self.nc, self.p
        TEN, V, A, G, Y, PO = nc.tensor, nc.vector, nc.scalar, nc.gpsimd, nc.sync, nc.gpsimd
        SC = nc.scalar
        with contextlib.ExitStack() as ph:
            ent = ph.enter_context
            wqk = ent(nc.sbuf_tensor(self.uname("wqk"), [128, 8192], BF16))
            wv = ent(nc.sbuf_tensor(self.uname("wv"), [128, 4096], BF16))
            h = ent(nc.sbuf_tensor(self.uname("hq"), [128, 2 * 4096], BF16))
            tb = ent(nc.sbuf_tensor(self.uname("tbq"), [128, 1024], F32))
            qraw = ent(nc.sbuf_tensor(self.uname("qraw"), [128, 2048], F32))
            qn = ent(nc.sbuf_tensor(self.uname("qn"), [128, 512], F32))
            t1b = ent(nc.sbuf_tensor(self.uname("t1b"), [128, 512], F32))
            qf = ent(nc.sbuf_tensor(self.uname("qf"), [128, 1024], BF16))
            vst = ent(nc.sbuf_tensor(self.uname("vstb"), [128, 512], BF16))
            gate = ent(nc.sbuf_tensor(self.uname("gateb"), [128, 512], BF16))
            sqn = ent(nc.sbuf_tensor(self.uname("sqn"), [128, 512], F32))
            nf = sqn
            nb = nf
            indqs = ent(nc.sbuf_tensor(self.uname("indqs"), [8, 512], F32))
            indks = ent(nc.sbuf_tensor(self.uname("indks"), [8, 512], F32))
            st = dict(indqs=indqs, indks=indks)
            self.emit_layer_prep(l, st)
            for oc in range(8):
                wq_ld = p.gdma("g_wqk", wqk[:, oc * 1024:(oc + 1) * 1024]
                               .rearrange("p (kc c) -> p kc c", kc=8),
                               self.inp["wqkv"][l][:, oc * 128:(oc + 1) * 128]
                               .rearrange("(kc p) c -> p kc c", p=128))
            for kc in range(8):
                wv_ld = p.gdma("g_wv", wv[:, kc * 512:(kc + 1) * 512],
                               self.inp["wqkv"][l][kc * 128:(kc + 1) * 128, 1024:1536])
            p.wait(nc.vector, "dv")
            h_free = {0: 0, 1: 0}
            tbl_free = [0, 0]
            ev_free = {}
            pb_free = 0
            qn_free = 0
            for t in range(NT):
                bi = t % 2
                c0 = t * TT
                hb = h[:, bi * 4096:bi * 4096 + 4096]
                hc = self.chunks(hb)
                p.wait(V, "pe", h_free[bi])
                last = None
                for c in range(NCH):
                    yt = self.ytile(c, c0) if c < 4 else self.ytile(c, c0, shift=1)
                    last = V.tensor_scalar(out=hc[c], in0=yt, scalar1=2.0, scalar2=None,
                                           op0=OP.mult)
                hdone = p.inc(last, "dv")
                tbs = [tb[:, 0:TT], tb[:, TT:2 * TT]]
                p.waitt(TEN, wv_ld)
                p.wait(TEN, "dv", hdone)
                for sub in range(4):
                    pv = self.bank(6 + sub % 2)
                    p.wait(TEN, "ac", ev_free.get(("v", sub % 2), 0))
                    for kc in range(8):
                        mm = TEN.matmul(pv, hb[:, kc * TT + sub * 128: kc * TT + sub * 128 + 128],
                                        wv[:, kc * 512:(kc + 1) * 512],
                                        start=(kc == 0), stop=(kc == 7))
                    pev = p.inc(mm, "pe")
                    p.wait(A, "pe", pev)
                    vs = vst[:, :]
                    p.waitt(A, ev_free.get("vw", ("pe", 0)))
                    p.wait(A, "dv", ev_free.get("vg", 0))
                    ev = p.inc(A.activation(out=vs, in_=pv, func=AF.Copy), "ac")
                    ev_free[("v", sub % 2)] = ev
                    p.wait(SC, "ac", ev)
                    jc = t * 4 + sub
                    ev_free["vw"] = p.dma(SC, f"q_vw{sub % 2}", self.vd[jc], vs)
                    if t == NT - 1:
                        p.wait(V, "ac", ev)
                        p.waitt(V, ev_free.get("hv", ("pe", 0)))
                        dg = p.inc(V.tensor_scalar(out=gate[:, :], in0=vs, scalar1=self.wgate,
                                                   scalar2=None, op0=OP.mult), "dv")
                        ev_free["vg"] = dg
                        p.wait(G, "dv", dg)
                        ev_free["hv"] = p.gdma("g_hv", self.vhalo_in[sub], gate[:, :])
                p.wait(TEN, "dv", hdone)
                for qk in range(2):
                    nbank = self.bank(2 + qk)
                    ind = indqs if qk == 0 else indks
                    cs, ss = tbs[0], tbs[1]
                    p.wait(Y, "dv", tbl_free[0])
                    p.wait(Y, "po", tbl_free[1])
                    tl = p.dma(Y, "q_tb", cs, self.inp["csq" if qk == 0 else "csk"][:, c0:c0 + TT])
                    tl = p.dma(Y, "q_tb", ss, self.inp["ssq" if qk == 0 else "ssk"][:, c0:c0 + TT])
                    for oc in range(4):
                        pqk = self.bank(oc % 2)
                        p.waitt(TEN, wq_ld)
                        p.wait(TEN, "ac", ev_free.get(("qk", oc % 2), 0))
                        woc = 4 * qk + oc
                        for kc in range(8):
                            mm = TEN.matmul(pqk,
                                            wqk[:, (woc * 8 + kc) * 128:(woc * 8 + kc) * 128 + 128],
                                            hc[kc], start=(kc == 0), stop=(kc == 7))
                        peq = p.inc(mm, "pe")
                        p.wait(A, "pe", peq)
                        sq = p.inc(A.activation(out=sqn[:, :], in_=pqk, func=AF.Square), "ac")
                        p.wait(TEN, "ac", sq)
                        mm = TEN.matmul(nbank[0:8, :], self.ind8Ts[:, oc * 8:oc * 8 + 8],
                                        sqn[:, :], start=(oc == 0), stop=(oc == 3))
                        p.inc(mm, "pe")
                        qro = qraw[:, oc * 512:(oc + 1) * 512]
                        p.wait(A, "dv", ev_free.get(("qraw", oc), 0))
                        ev = p.inc(A.activation(out=qro, in_=pqk, func=AF.Copy), "ac")
                        ev_free[("qk", oc % 2)] = ev
                    p.wait(A, "pe")
                    p.wait(A, "ac")
                    a1 = p.inc(A.activation(out=nf[0:8, :], in_=nbank[0:8, :], func=AF.Ln,
                                            bias=1e-18, scale=1.0), "ac")
                    p.wait(A, "ac", a1)
                    a2 = p.inc(A.activation(out=nb[0:8, :], in_=nf[0:8, :], func=AF.Exp,
                                            scale=-0.5), "ac")
                    for oc in range(4):
                        woc = 4 * qk + oc
                        qro = qraw[:, oc * 512:(oc + 1) * 512]
                        pb = self.bank(4)
                        pr = self.bank(5)
                        p.wait(TEN, "ac", a2)
                        p.wait(TEN, "dv", pb_free)
                        mm = TEN.matmul(pb, ind[0:8, oc * 128:(oc + 1) * 128], nb[0:8, :],
                                        start=True, stop=True)
                        peb = p.inc(mm, "pe")
                        p.wait(V, "pe", peb)
                        p.wait(V, "ac", ev_free.get(("qk", oc % 2), 0))
                        p.wait(V, "po", qn_free)
                        d1 = p.inc(V.tensor_tensor(out=qn[:, :], in0=qro, in1=pb, op=OP.mult), "dv")
                        pb_free = d1
                        p.wait(TEN, "dv", d1)
                        mm = TEN.matmul(pr, self.PTf[:, :], qn[:, :], start=True, stop=True)
                        per = p.inc(mm, "pe")
                        p.wait(V, "pe", per)
                        p.waitt(V, tl)
                        d2 = p.inc(V.tensor_tensor(out=t1b[:, :], in0=pr, in1=ss, op=OP.mult), "dv")
                        p.wait(PO, "dv", d1)
                        p.waitt(PO, tl)
                        d3 = p.inc(PO.tensor_tensor(out=qro, in0=qn[:, :], in1=cs,
                                                    op=OP.mult), "po")
                        qn_free = d3
                        p.wait(V, "dv", d2)
                        p.wait(V, "po", d3)
                        qfo = qf[:, (oc % 2) * 512:(oc % 2) * 512 + 512]
                        p.waitt(V, ev_free.get(("qf", oc % 2), ("pe", 0)))
                        d4 = p.inc(V.tensor_tensor(out=qfo, in0=t1b[:, :], in1=qro,
                                                   op=OP.add), "dv")
                        ev_free[("qraw", oc)] = d4
                        p.wait(SC, "dv", d4)
                        if qk == 0:
                            ev_free[("qf", oc % 2)] = p.dma(SC, f"q_qw{oc % 2}",
                                                            self.qd[oc][:, c0:c0 + TT], qfo)
                        else:
                            ev_free[("qf", oc % 2)] = p.dma(
                                SC, f"q_kw{oc % 2}",
                                self.kd[t * 4:t * 4 + 4, oc].rearrange("j p c -> p j c"),
                                qfo.rearrange("p (j c) -> p j c", j=4))
                            if t == NT - 1:
                                p.wait(V, "dv", d4)
                                p.waitt(V, ev_free.get("hk", ("pe", 0)))
                                p.waitt(V, ev_free.get("hv", ("pe", 0)))
                                p.waitt(V, ev_free[("qf", oc % 2)])
                                dg = p.inc(V.tensor_scalar(out=gate[:, :], in0=qfo,
                                                           scalar1=self.wgate, scalar2=None,
                                                           op0=OP.mult), "dv")
                                p.wait(G, "dv", dg)
                                ev_free["hk"] = p.gdma(
                                    "g_hk", self.khalo_in[:, oc].rearrange("j p c -> p j c"),
                                    gate[:, :].rearrange("p (j c) -> p j c", j=4))
                    tbl_free[0] = p.cnt["dv"]
                    tbl_free[1] = p.cnt["po"]
                h_free[bi] = p.cnt["pe"]
            p.wait(G, "cc")
            for tok in (ev_free.get("hv"), ev_free.get("hk")):
                if tok is not None:
                    p.waitt(G, tok)
            p.inc(G.collective_compute("AllReduce", OP.add, replica_groups=RG,
                                       ins=[self.khalo_in], outs=[self.khalo]), "cc")
            cc = p.inc(G.collective_compute("AllReduce", OP.add, replica_groups=RG,
                                            ins=[self.vhalo_in], outs=[self.vhalo]), "cc")
            self.fence()
        return cc

    def emit_ffn_attn(self, l):
        nc, p = self.nc, self.p
        TEN, V, A, G, Y, PO = nc.tensor, nc.vector, nc.scalar, nc.gpsimd, nc.sync, nc.gpsimd
        SC = nc.scalar
        with contextlib.ExitStack() as ph:
            ent = ph.enter_context
            masksb = ent(nc.sbuf_tensor(self.uname("masksb"), [128, 25 * 128], BF16))
            self.masksb = masksb
            wos = ent(nc.sbuf_tensor(self.uname("wos"), [128, 4096], BF16))
            qst = ent(nc.sbuf_tensor(self.uname("qst"), [128, 2 * 2048], BF16))
            kst = ent(nc.sbuf_tensor(self.uname("kst"), [128, 8 * 512], BF16))
            vpl = ent(nc.sbuf_tensor(self.uname("vpl"), [128, 8 * 520], BF16))
            vtmp = ent(nc.sbuf_tensor(self.uname("vtmp"), [128, 512], BF16))
            asb = ent(nc.sbuf_tensor(self.uname("asb"), [128, 2 * 640], BF16))
            obf = ent(nc.sbuf_tensor(self.uname("obf"), [128, 2048], F32))
            ofin = ent(nc.sbuf_tensor(self.uname("ofin"), [128, 2048], BF16))
            zr8 = ent(nc.sbuf_tensor(self.uname("zr8"), [8, 512], F32))
            rz8 = ent(nc.sbuf_tensor(self.uname("rz8"), [8, 512], F32))
            self._zslot_free = {}
            self._zr_free = 0
            self._rz_free = 0
            mld = p.dma(Y, "q_mask", masksb[:, :].rearrange("p (a b) -> p a b", a=25),
                        self.inp["masks"].rearrange("x y p q -> p (x y) q"))
            for oc in range(4):
                wld = p.gdma("g_wo", wos[:, oc * 1024:(oc + 1) * 1024]
                             .rearrange("p (xc c) -> p xc c", xc=8),
                             self.inp["wo"][l][oc * 128:(oc + 1) * 128, :]
                             .rearrange("p (xc c) -> p xc c", xc=8))
            stA = self.ps[0][:, 0:640]
            stB = self.ps[1][:, 0:640]
            oA = self.ps[2][0:65, 0:512]
            oB = self.ps[2][0:65, 512:1024]
            zb = self.ps[0][:, 0:512]
            slot_free = {}
            q_free = {}
            st_free = {0: 0, 1: 0}
            asb_free = {0: 0, 1: 0}
            o_free = 0
            zb_free = 0
            psx_free = {0: 0, 1: 0}
            obf_free = 0
            ofin_free = 0
            loaded = {}
            self._kld = {}
            self._kgd = {}
            self._vtmp_free = 0
            worder = [7, 0, 1, 2, 3, 4, 5, 6]
            for wi, cqg in enumerate(worder):
                qb = wi % 2
                qsl = qst[:, qb * 2048:(qb + 1) * 2048]
                p.wait(Y, "pe", q_free.get(qb, 0))
                qld = p.dma(Y, f"q_qs{qb}", qsl.rearrange("p (c t) -> p c t", c=4),
                            self.qd[:, :, cqg * 512:(cqg + 1) * 512].rearrange("c p t -> p c t"))
                for j in range(4 * cqg - 4, 4 * cqg + 4):
                    sl = j % 8
                    if loaded.get(sl) == j:
                        continue
                    loaded[sl] = j
                    p.wait(Y, "pe", slot_free.get(sl, 0))
                    p.wait(G, "pe", slot_free.get(sl, 0))
                    ksrc = self.kd[j] if j >= 0 else self.khalo[j + 4]
                    vsrc = self.vd[j] if j >= 0 else self.vhalo[j + 4]
                    self._kld[sl] = p.dma(Y, f"q_ks{sl}", kst[:, sl * 512:(sl + 1) * 512]
                                          .rearrange("p (c x) -> p c x", c=4),
                                          ksrc.rearrange("c p x -> p c x"))
                    if j < 0:
                        p.waitt(V, self._kld[sl])
                        p.wait(V, "pe", slot_free.get(sl, 0))
                        kgd = p.inc(V.tensor_scalar(out=kst[:, sl * 512:(sl + 1) * 512],
                                                    in0=kst[:, sl * 512:(sl + 1) * 512],
                                                    scalar1=self.rgate, scalar2=None,
                                                    op0=OP.mult), "dv")
                        self._kgd[sl] = kgd
                    p.wait(G, "dv", self._vtmp_free)
                    vl = p.dma(G, f"g_vs{sl}", vtmp[:, :], vsrc)
                    p.waitt(V, vl)
                    p.wait(V, "pe", slot_free.get(sl, 0))
                    vslot = vpl[:, sl * 520:(sl + 1) * 520]
                    m1 = p.inc(V.memset(vslot, 1.0), "dv")
                    p.wait(V, "dv", m1)
                    self._vtmp_free = p.inc(
                        V.tensor_copy(out=vslot.rearrange("p (h c) -> p h c", h=8)[:, :, 0:64],
                                      in_=vtmp[:, :].rearrange("p (h c) -> p h c", h=8)), "dv")
                vload_dv = p.cnt["dv"]
                if "sdbg" in self.taps and wi == 0:
                    dq = self.nc.dram_tensor("dbg_qsl", [128, 2048], BF16,
                                             kind="ExternalOutput").ap()
                    dk = self.nc.dram_tensor("dbg_kst", [128, 4096], BF16,
                                             kind="ExternalOutput").ap()
                    p.waitt(Y, qld)
                    for s2 in range(8):
                        p.waitt(Y, self._kld[s2])
                    p.wait(Y, "dv", vload_dv)
                    p.dma(Y, "q_tap", dq, qsl)
                    p.dma(Y, "q_tap", dk, kst[:, :])
                for c in range(4):
                    hA, hB = 2 * c, 2 * c + 1
                    for cq in range(4):
                        cqglob = cqg * 4 + cq
                        p.waitt(TEN, qld)
                        for cr in range(5):
                            p.waitt(TEN, self._kld[(cqglob - 4 + cr) % 8])
                            p.wait(TEN, "dv", self._kgd.get((cqglob - 4 + cr) % 8, 0))
                        p.wait(TEN, "ac", st_free[0])
                        p.wait(TEN, "ac", st_free[1])
                        p.wait(TEN, "dv", zb_free)
                        for cr in range(5):
                            j = cqglob - 4 + cr
                            sl = j % 8
                            rhA = qsl[0:64, c * 512 + cq * 128: c * 512 + cq * 128 + 128]
                            rhB = qsl[64:128, c * 512 + cq * 128: c * 512 + cq * 128 + 128]
                            lhA = kst[0:64, sl * 512 + c * 128: sl * 512 + c * 128 + 128]
                            lhB = kst[64:128, sl * 512 + c * 128: sl * 512 + c * 128 + 128]
                            TEN.matmul(stA[:, cr * 128:(cr + 1) * 128], lhA, rhA,
                                       start=True, stop=True, tile_position=(0, 0))
                            mm = TEN.matmul(stB[:, cr * 128:(cr + 1) * 128], lhB, rhB,
                                            start=True, stop=True, tile_position=(64, 0))
                        peS = p.inc(mm, "pe")
                        slot_free.update({(cqglob - 4 + cr) % 8: peS for cr in range(5)})
                        q_free[qb] = peS
                        if "sdbg" in self.taps and wi == 0 and c == 0 and cq == 0:
                            dst_d = nc.dram_tensor("dbg_st", [128, 1280], F32,
                                                   kind="ExternalOutput").ap()
                            dng = nc.dram_tensor("dbg_nc", [128, 1], F32,
                                                 kind="ExternalOutput").ap()
                            p.wait(A, "pe", peS)
                            p.inc(A.activation(out=ofin[:, 0:640], in_=stA, func=AF.Copy), "ac")
                            x1 = p.inc(A.activation(out=ofin[:, 640:1280], in_=stB,
                                                    func=AF.Copy), "ac")
                            p.wait(G, "ac", x1)
                            tk1 = p.gdma("g_tap", dst_d[:, 0:640], ofin[:, 0:640])
                            tk1 = p.gdma("g_tap", dst_d[:, 640:1280], ofin[:, 640:1280])
                            tk2 = p.gdma("g_tap", dng, self.negC)
                            p.waitt(V, tk1)
                            p.waitt(A, tk1)
                        p.wait(A, "pe", peS)
                        p.wait(A, "pe", asb_free[0])
                        p.wait(A, "pe", asb_free[1])
                        eA = p.inc(A.activation(out=asb[:, 0:640], in_=stA, func=AF.Exp,
                                                bias=self.negC, scale=1.0), "ac")
                        eB = p.inc(A.activation(out=asb[:, 640:1280], in_=stB, func=AF.Exp,
                                                bias=self.negC, scale=1.0), "ac")
                        st_free[0], st_free[1] = eA, eB
                        cls = cqglob if cqglob < 4 else 4
                        crs = range(5) if cqglob < 4 else (0, 4)
                        p.waitt(V, mld)
                        p.wait(V, "ac", eB)
                        last = None
                        for cr in crs:
                            mk = self.mask(cls, cr)
                            last = V.tensor_tensor(out=asb[:, cr * 128:(cr + 1) * 128],
                                                   in0=asb[:, cr * 128:(cr + 1) * 128],
                                                   in1=mk, op=OP.mult)
                            last = V.tensor_tensor(out=asb[:, 640 + cr * 128:640 + (cr + 1) * 128],
                                                   in0=asb[:, 640 + cr * 128:640 + (cr + 1) * 128],
                                                   in1=mk, op=OP.mult)
                        dvM = p.inc(last, "dv")
                        p.wait(TEN, "dv", dvM)
                        p.wait(TEN, "dv", vload_dv)
                        p.wait(TEN, "ac", o_free)
                        for cr in range(5):
                            j = cqglob - 4 + cr
                            sl = j % 8
                            TEN.matmul(oA[:, cq * 128:(cq + 1) * 128],
                                       vpl[:, sl * 520 + hA * 65: sl * 520 + hA * 65 + 65],
                                       asb[:, cr * 128:(cr + 1) * 128],
                                       start=(cr == 0), stop=(cr == 4))
                            mm = TEN.matmul(oB[:, cq * 128:(cq + 1) * 128],
                                            vpl[:, sl * 520 + hB * 65: sl * 520 + hB * 65 + 65],
                                            asb[:, 640 + cr * 128:640 + (cr + 1) * 128],
                                            start=(cr == 0), stop=(cr == 4))
                        peav = p.inc(mm, "pe")
                        asb_free[0] = asb_free[1] = peav
                        slot_free.update({(cqglob - 4 + cr) % 8: peav for cr in range(5)})
                    p.wait(A, "pe", peav)
                    p.wait(A, "dv", obf_free)
                    p.wait(A, "pe", ofin_free)
                    for hh, ob in ((hA, oA), (hB, oB)):
                        dst = obf[:, c * 512:(c + 1) * 512]
                        dsv = dst[0:64, :] if hh == hA else dst[64:128, :]
                        p.inc(A.activation(out=dsv, in_=ob[0:64, :], func=AF.Copy), "ac")
                        zs = self.row[:, (hh % 2) * 512:(hh % 2) * 512 + 512]
                        p.waitt(A, self._zslot_free.get(hh % 2, ("pe", 0)))
                        az = p.inc(A.activation(out=zs, in_=ob[64:65, :], func=AF.Copy), "ac")
                        p.wait(Y, "ac", az)
                        self._zslot_free[hh % 2] = p.dma(Y, f"q_zw{hh % 2}", self.zrowd[hh], zs)
                    o_free = p.cnt["ac"]
                p.wait(Y, "q_zw0")
                p.wait(Y, "q_zw1")
                p.wait(Y, "ac", self._zr_free)
                p.wait(Y, "pe", self._rz_free)
                zl = p.dma(Y, "q_zr", zr8[:, :], self.zrowd)
                p.waitt(A, zl)
                a1 = p.inc(A.activation(out=zr8[:, :], in_=zr8[:, :], func=AF.Ln), "ac")
                p.wait(A, "ac", a1)
                p.wait(A, "pe", self._rz_free)
                a2 = p.inc(A.activation(out=rz8[:, :], in_=zr8[:, :], func=AF.Exp,
                                        scale=-1.0), "ac")
                self._zr_free = a2
                dlast = None
                for c in range(4):
                    p.wait(TEN, "ac", a2)
                    p.wait(TEN, "dv", zb_free)
                    mm = TEN.matmul(zb, self.ind8s[0:8, c * 128:(c + 1) * 128], rz8[:, :],
                                    start=True, stop=True)
                    pez = p.inc(mm, "pe")
                    self._rz_free = pez
                    p.wait(V, "pe", pez)
                    p.wait(V, "ac", o_free)
                    p.wait(V, "pe", ofin_free)
                    dlast = p.inc(V.tensor_tensor(out=ofin[:, c * 512:(c + 1) * 512],
                                                  in0=obf[:, c * 512:(c + 1) * 512],
                                                  in1=zb, op=OP.mult), "dv")
                    zb_free = dlast
                obf_free = dlast
                if "odbg" in self.taps:
                    if not hasattr(self, "_odbg"):
                        self._odbg = nc.dram_tensor("dbg_o", [8, 128, 2048], BF16,
                                                    kind="ExternalOutput").ap()
                        self._obdbg = nc.dram_tensor("dbg_ob", [8, 128, 2048], BF16,
                                                     kind="ExternalOutput").ap()
                        self._zdbg = nc.dram_tensor("dbg_z", [8, 8, 512], F32,
                                                    kind="ExternalOutput").ap()
                    p.wait(Y, "dv", zb_free)
                    p.wait(Y, "ac", o_free)
                    otap = p.dma(Y, "q_tapA", self._odbg[cqg], ofin[:, :])
                    p.waitt(V, otap)
                    otap2 = p.dma(Y, "q_tapB", self._obdbg[cqg], obf[:, :])
                    p.waitt(V, otap2)
                    p.waitt(A, p.dma(Y, "q_tapC", self._zdbg[cqg], zr8[:, :]))
                p.waitt(TEN, wld)
                c0 = cqg * TT
                for xc in range(NCH):
                    px = self.ps[3][:, (xc % 2) * 512:(xc % 2) * 512 + 512]
                    p.wait(TEN, "dv", max(zb_free, psx_free[xc % 2]))
                    for oc in range(4):
                        mm = TEN.matmul(px, wos[:, (oc * 8 + xc) * 128:(oc * 8 + xc) * 128 + 128],
                                        ofin[:, oc * 512:(oc + 1) * 512],
                                        start=(oc == 0), stop=(oc == 3))
                    pex = p.inc(mm, "pe")
                    ofin_free = pex
                    p.wait(V, "pe", pex)
                    yt = self.ytile(xc, c0)
                    dyu = p.inc(V.scalar_tensor_tensor(out=yt, in0=px, scalar=0.5, in1=yt,
                                                       op0=OP.mult, op1=OP.add), "dv")
                    psx_free[xc % 2] = dyu
                if cqg == 7:
                    self.emit_ycol_exchange()
            self.fence()

    def emit_ffn_mlp(self, l):
        nc, p = self.nc, self.p
        TEN, V, A, G, Y = nc.tensor, nc.vector, nc.scalar, nc.gpsimd, nc.sync
        NJ = 22
        segs = [(0, 6), (6, 12), (12, 18), (18, 22)]
        mof = lambda jj: 128 if jj < NJ - 1 else 42
        with contextlib.ExitStack() as ph:
            ent = ph.enter_context
            gcol = ent(nc.sbuf_tensor(self.uname("m_gc"), [128, 64], F32))
            gb2 = ent(nc.sbuf_tensor(self.uname("m_gb2"), [2, 1024], BF16))
            h1t = [ent(nc.sbuf_tensor(self.uname(f"m_h1{i}"), [128, NCH * TT], BF16)) for i in range(2)]
            pha = contextlib.ExitStack()
            enta = pha.enter_context
            hb = enta(nc.sbuf_tensor(self.uname("m_hb"), [128, NCH * TT], BF16))
            sqb = enta(nc.sbuf_tensor(self.uname("m_sq"), [128, 2 * TT], BF16))
            rr2 = enta(nc.sbuf_tensor(self.uname("m_rr2"), [2, 512], BF16))
            t0 = enta(nc.sbuf_tensor(self.uname("m_t0"), [128, TT], F32))
            nbr = self.row[:, 0:1024]
            pl = p.dma(Y, "q_gb", nbr, self.inp["ln_b"][l:l + 1, :])
            p.waitt(V, pl)
            p.wait(V, "dv")
            nb1 = p.inc(V.tensor_scalar(out=nbr, in0=nbr, scalar1=-1.0, scalar2=None,
                                        op0=OP.mult), "dv")
            p.wait(G, "dv", nb1)
            gw = p.gdma("g_gb", self.gbd[1:2, :], nbr)
            with nc.allow_non_contiguous_dma(reason="one-time per-layer column loads"):
                gl = p.dma(Y, "q_gc", gcol[:, 0:8],
                           self.inp["ln_g"][l].rearrange("(c p) -> p c", p=128))
                p.waitt(Y, gw)
                g2a = p.gdma("g_gb2", gb2[0:1, :], self.inp["ln_g"][l:l + 1, :])
                g2b = p.dma(Y, "q_gb2b", gb2[1:2, :], self.gbd[1])
                bl1 = p.dma(Y, "q_gc", gcol[:, 8:8 + 21],
                            self.inp["b1"][l][0:2688].rearrange("(c p) -> p c", p=128))
                bl2 = p.dma(Y, "q_gc", gcol[0:42, 29:30], self.inp["b1"][l][2688:2730]
                            .rearrange("(c p) -> p c", p=42))
                bl3 = p.dma(Y, "q_gc", gcol[:, 30:30 + 21],
                            self.inp["b1"][l][2730:2730 + 2688].rearrange("(c p) -> p c", p=128))
                bl4 = p.dma(Y, "q_gc", gcol[0:42, 51:52], self.inp["b1"][l][2730 + 2688:5460]
                            .rearrange("(c p) -> p c", p=42))
            gc_all = ("q_gc", p.cnt["q_gc"])
            p.inc(V.memset(rr2[:, :], 1.0), "dv")
            b1a = lambda jj, m: gcol[0:m, 8 + jj:9 + jj]
            b1g = lambda jj, m: gcol[0:m, 30 + jj:31 + jj]
            p.waitt(V, gw)
            s1 = self.bank(0)[0:1, :]
            s2 = self.bank(1)[0:1, :]
            rb = self.bank(2)
            tbps = [self.bank(3), self.bank(4)]
            dv_prev = 0
            pe_prev = 0
            peb2 = 0
            h1w = {}
            sq_free = {0: 0, 1: 0}
            dvh_par = {0: 0, 1: 0}
            for t in [1, 2, 3, 4, 5, 6, 7, 0]:
                c0 = t * TT
                hc = self.chunks(hb)
                p.wait(V, "pe", pe_prev)
                p.wait(V, "dv")
                ctok = []
                for c in range(NCH):
                    yt = self.ytile(c, c0) if c < 4 else self.ytile(c, c0, shift=1)
                    cp = V.tensor_scalar(out=hc[c], in0=yt, scalar1=1.0, scalar2=None,
                                         op0=OP.mult)
                    ctok.append(p.inc(cp, "dv"))
                p.wait(TEN, "dv", dv_prev)
                p.wait(TEN, "pe", pe_prev)
                for c in range(NCH):
                    sl = sqb[:, (c % 2) * TT:(c % 2) * TT + TT]
                    p.wait(A, "pe", sq_free[c % 2])
                    p.wait(A, "dv", ctok[c])
                    aq = p.inc(A.activation(out=sl, in_=hc[c], func=AF.Square), "ac")
                    p.wait(TEN, "dv", ctok[c])
                    TEN.matmul(s1, self.ones128b, hc[c], start=(c == 0), stop=(c == NCH - 1))
                    p.wait(TEN, "ac", aq)
                    mm = TEN.matmul(s2, self.ones128b, sl, start=(c == 0), stop=(c == NCH - 1))
                    sq_free[c % 2] = p.inc(mm, "pe")
                pe_stats = p.cnt["pe"]
                r = self.row
                mu, t1r, t2r = (r[:, i * TT:(i + 1) * TT] for i in range(3))
                p.wait(V, "pe", pe_stats)
                d1 = p.inc(V.tensor_scalar(out=mu, in0=s1, scalar1=1.0 / D, scalar2=None,
                                           op0=OP.mult), "dv")
                p.wait(V, "dv", d1)
                d2 = p.inc(V.tensor_tensor(out=t1r, in0=mu, in1=mu, op=OP.mult), "dv")
                p.wait(V, "dv", d2)
                d3 = p.inc(V.scalar_tensor_tensor(out=t1r, in0=s2, scalar=1.0 / D, in1=t1r,
                                                  op0=OP.mult, op1=OP.subtract), "dv")
                p.wait(A, "dv", d3)
                a1 = p.inc(A.activation(out=t2r, in_=t1r, func=AF.Ln, bias=1e-5, scale=1.0), "ac")
                p.wait(A, "ac", a1)
                a2 = p.inc(A.activation(out=t2r, in_=t2r, func=AF.Exp, scale=-0.5), "ac")
                rp = t2r
                p.wait(V, "ac", a2)
                d4 = p.inc(V.tensor_tensor(out=t1r, in0=mu, in1=rp, op=OP.mult), "dv")
                sp = t1r
                p.wait(V, "dv", d4)
                p.wait(V, "pe", peb2)
                d5 = p.inc(V.tensor_copy(out=rr2[0:1, :], in_=sp), "dv")
                p.wait(TEN, "ac", a2)
                p.wait(TEN, "dv", d5)
                mmb = TEN.matmul(rb, self.onesrow[:, :], rp, start=True, stop=True)
                pe_b = p.inc(mmb, "pe")
                pe_prev = pe_b
                bi = t % 2
                hh = h1t[bi]
                h1c = [hh[:, c * TT:(c + 1) * TT] for c in range(NCH)]
                p.waitt(V, h1w.get(bi, ("pe", 0)))
                p.waitt(V, gc_all)
                for c in range(NCH):
                    tb = tbps[c % 2]
                    p.waitt(TEN, g2a)
                    p.waitt(TEN, g2b)
                    p.wait(TEN, "dv", dvh_par[c % 2])
                    mm2 = TEN.matmul(tb[:, 0:TT], gb2[:, c * 128:(c + 1) * 128], rr2[:, :],
                                     start=True, stop=True)
                    peb2 = p.inc(mm2, "pe")
                    p.wait(V, "pe", peb2)
                    dt0 = p.inc(V.tensor_tensor(out=t0[:, :], in0=hc[c], in1=rb, op=OP.mult), "dv")
                    p.wait(V, "dv", dt0)
                    last = V.scalar_tensor_tensor(out=h1c[c], in0=t0[:, :],
                                                  scalar=gcol[:, c:c + 1], in1=tb[:, 0:TT],
                                                  op0=OP.mult, op1=OP.subtract)
                    dvh_par[c % 2] = p.inc(last, "dv")
                dv_prev = p.cnt["dv"]
                p.wait(Y, "dv", dv_prev)
                h1w[bi] = p.dma(Y, f"q_h1w{bi}", self.fm(self.h1d, c0),
                                hh[:, :].rearrange("p (c t) -> p c t", c=NCH))
            self.fence()
            pha.close()
            phb = contextlib.ExitStack()
            entb = phb.enter_context
            b2r = entb(nc.sbuf_tensor(self.uname("m_b2r"), [1, 1024], BF16))
            gsb = entb(nc.sbuf_tensor(self.uname("m_gsb"), [128, 2 * TT], F32))
            h2s = entb(nc.sbuf_tensor(self.uname("m_h2"), [128, 6 * TT], BF16))
            w1s = entb(nc.sbuf_tensor(self.uname("m_w1"), [128, 12 * 1024], BF16))
            w2s = entb(nc.sbuf_tensor(self.uname("m_w2"), [128, 6 * 1024], BF16))
            b2l = p.gdma("g_b2", b2r[:, :], self.inp["b2"][l:l + 1, :])
            h1_free = {0: 0, 1: 0}
            pa_free = {0: 0, 1: 0, 2: 0}
            pg_free = {0: 0, 1: 0, 2: 0}
            px_free = {0: 0, 1: 0}
            h2_free = 0
            gsb_free = {0: 0, 1: 0}
            w_free = 0
            w1sl = lambda sl, kc, m: w1s[:, sl * 2048 + kc * m: sl * 2048 + (kc + 1) * m]
            w1gl = lambda sl, kc, m: w1s[:, sl * 2048 + 1024 + kc * m: sl * 2048 + 1024 + (kc + 1) * m]
            for si, (j0, j1) in enumerate(segs):
                p.wait(G, "pe", w_free)
                jtok = {}
                for jj in range(j0, j1):
                    m = mof(jj)
                    sl = jj - j0
                    p.gdma(f"g_wj{sl}", w1s[:, sl * 2048:sl * 2048 + 8 * m]
                           .rearrange("p (kc c) -> p kc c", kc=8),
                           self.inp["w1"][l][:, jj * 128:jj * 128 + m]
                           .rearrange("(kc p) c -> p kc c", p=128))
                    p.gdma(f"g_wj{sl}", w1s[:, sl * 2048 + 1024:sl * 2048 + 1024 + 8 * m]
                           .rearrange("p (kc c) -> p kc c", kc=8),
                           self.inp["w1"][l][:, HID + jj * 128:HID + jj * 128 + m]
                           .rearrange("(kc p) c -> p kc c", p=128))
                    tok = p.gdma(f"g_wj{sl}", w2s[0:m, sl * 1024:(sl + 1) * 1024]
                                 .rearrange("p (xc c) -> p xc c", xc=8),
                                 self.inp["w2"][l][jj * 128:jj * 128 + m, :]
                                 .rearrange("p (xc c) -> p xc c", xc=8))
                    jtok[sl] = tok
                for t in [7, 0, 1, 2, 3, 4, 5, 6]:
                    c0 = t * TT
                    bi = t % 2
                    hh = h1t[bi]
                    h1c = [hh[:, c * TT:(c + 1) * TT] for c in range(NCH)]
                    p.wait(Y, "pe", h1_free[bi])
                    p.waitt(Y, h1w[bi])
                    hl = p.dma(Y, f"q_h1r{bi}", hh[:, :].rearrange("p (c t) -> p c t", c=NCH),
                               self.fm(self.h1d, c0))
                    p.waitt(TEN, hl)
                    dh2_sl = {}
                    for jj in range(j0, j1):
                        m = mof(jj)
                        sl = jj - j0
                        par = sl % 3
                        gpar = sl % 2
                        pa = self.bank((0, 1, 6)[par])
                        pg = self.bank((2, 3, 7)[par])
                        gsl = gsb[0:m, gpar * TT:gpar * TT + TT]
                        p.waitt(TEN, jtok[sl])
                        p.wait(TEN, "dv", pa_free[par])
                        p.wait(TEN, "ac", pg_free[par])
                        p.wait(TEN, "pe", h2_free if t != 7 or si > 0 else 0)
                        for kc in range(8):
                            TEN.matmul(pa[0:m, :], w1sl(sl, kc, m), h1c[kc],
                                       start=(kc == 0), stop=(kc == 7))
                        for kc in range(8):
                            mm = TEN.matmul(pg[0:m, :], w1gl(sl, kc, m), h1c[kc],
                                            start=(kc == 0), stop=(kc == 7))
                        peg = p.inc(mm, "pe")
                        p.wait(A, "pe", peg)
                        p.waitt(A, gc_all)
                        p.waitt(A, gc_all)
                        p.wait(A, "dv", gsb_free[gpar])
                        ga = p.inc(A.activation(out=gsl, in_=pg[0:m, :], func=AF.Gelu,
                                                bias=b1g(jj, m), scale=1.0), "ac")
                        pg_free[par] = ga
                        p.wait(V, "ac", ga)
                        p.waitt(V, gc_all)
                        p.wait(V, "pe", h2_free)
                        dh2 = p.inc(V.scalar_tensor_tensor(out=h2s[0:m, sl * TT:(sl + 1) * TT],
                                                           in0=pa[0:m, :], scalar=b1a(jj, m),
                                                           in1=gsl, op0=OP.add,
                                                           op1=OP.mult), "dv")
                        gsb_free[gpar] = dh2
                        pa_free[par] = dh2
                        dh2_sl[sl] = dh2
                    for xc in range(NCH):
                        par = xc % 2
                        px = self.bank(4 + par)
                        p.wait(TEN, "dv", px_free[par])
                        for jj in range(j0, j1):
                            m = mof(jj)
                            sl = jj - j0
                            p.waitt(TEN, jtok[sl])
                            p.wait(TEN, "dv", dh2_sl[sl])
                            stop = (si != 0 and jj == j1 - 1)
                            mm = TEN.matmul(px, w2s[0:m, sl * 1024 + xc * 128:sl * 1024 + (xc + 1) * 128],
                                            h2s[0:m, sl * TT:(sl + 1) * TT],
                                            start=(jj == j0), stop=stop)
                        if si == 0:
                            p.waitt(TEN, b2l)
                            mm = TEN.matmul(px, b2r[:, xc * 128:(xc + 1) * 128], self.onesb[:, :],
                                            start=False, stop=True)
                        pex = p.inc(mm, "pe")
                        h2_free = pex
                        p.wait(V, "pe", pex)
                        yt = self.ytile(xc, c0)
                        dyu = p.inc(V.scalar_tensor_tensor(out=yt, in0=px, scalar=0.5, in1=yt,
                                                           op0=OP.mult, op1=OP.add), "dv")
                        px_free[par] = dyu
                    h1_free[bi] = p.cnt["pe"]
                w_free = p.cnt["pe"]
            self.fence()
            phb.close()

    def bcast_col(self, scal, dst, stage2):
        nc, p = self.nc, self.p
        TEN, V, A = nc.tensor, nc.vector, nc.scalar
        p.wait(V, "dv")
        m0 = p.inc(V.memset(stage2, 0.0), "dv")
        p.wait(V, "dv", m0)
        m1 = p.inc(V.tensor_copy(out=stage2[0:1, 0:1], in_=scal), "dv")
        p.wait(TEN, "dv", m1)
        p.wait(TEN, "ac")
        pb = self.ps[3][:, 768:770]
        mm = TEN.matmul(pb, self.onesb[0:1, 0:128], stage2, start=True, stop=True)
        pe1 = p.inc(mm, "pe")
        p.wait(A, "pe", pe1)
        return p.inc(A.activation(out=dst, in_=pb[:, 0:1], func=AF.Copy), "ac")

    def emit_global_attn(self):
        nc, p = self.nc, self.p
        TEN, V, A, G, Y = nc.tensor, nc.vector, nc.scalar, nc.gpsimd, nc.sync
        SC = nc.scalar
        p.mksem("cck")
        p.mksem("ccv")
        yb = self.y[:, :].bitcast(BF16)
        q_all = lambda dc: yb[:, dc * T:(dc + 1) * T]
        vres = lambda jc: yb[:, jc * 1024:(jc + 1) * 1024]
        qm2 = self.smalls[0:1, 32:33]
        tmax = self.smalls[0:1, 33:34]
        with contextlib.ExitStack() as ph:
            ent = ph.enter_context
            wT = ent(nc.sbuf_tensor(self.uname("ga_wT"), [128, 2 * 64 * 128], BF16))
            wst = ent(nc.sbuf_tensor(self.uname("ga_wst"), [128, 2048], BF16))
            h2t = [ent(nc.sbuf_tensor(self.uname("ga_h2"), [128, NCH * TT], BF16))
                   for i in range(2)]
            qkt = ent(nc.sbuf_tensor(self.uname("ga_qk"), [128, 1024], BF16))
            sqn = ent(nc.sbuf_tensor(self.uname("ga_sqn"), [128, 1024], BF16))
            bqk = ent(nc.sbuf_tensor(self.uname("ga_bqk"), [128, 16], F32))
            pt = self.ps[3][:, 512:768].bitcast(BF16)
            tr_ev = 0
            for part in range(2):
                for fc in range(8):
                    wl = p.gdma(f"g_gw{fc % 2}", wst[:, (fc % 2) * 1024:(fc % 2) * 1024 + 1024],
                                self.inp["attn_W"][part * 1024 + fc * 128:
                                                   part * 1024 + fc * 128 + 128, :])
                    p.waitt(TEN, wl)
                    for dc in range(8):
                        p.wait(TEN, "ac", tr_ev)
                        mm = TEN.transpose(pt[:, 0:128],
                                           wst[:, (fc % 2) * 1024 + dc * 128:
                                               (fc % 2) * 1024 + dc * 128 + 128], self.ident)
                        pet = p.inc(mm, "pe")
                        p.wait(A, "pe", pet)
                        slot = part * 64 + fc * 8 + dc
                        tr_ev = p.inc(A.activation(out=wT[:, slot * 128:(slot + 1) * 128],
                                                   in_=pt[:, 0:128], func=AF.Copy), "ac")
                    p.wait(G, "pe", p.cnt["pe"])
            with nc.allow_non_contiguous_dma(reason="one-time bias col loads"):
                bq = p.dma(Y, "q_gab", bqk[:, 0:8],
                           self.inp["attn_b"][0, 0:1024].rearrange("(c p) -> p c", p=128))
                bq = p.dma(Y, "q_gab", bqk[:, 8:16],
                           self.inp["attn_b"][0, 1024:2048].rearrange("(c p) -> p c", p=128))
            s1 = self.bank(4)[0:1, :]
            nslot = lambda i: self.row[:, (i % 2) * 512:(i % 2) * 512 + 512]
            ev_free = {}
            h_free = {0: 0, 1: 0}
            n_free = {}
            p.wait(TEN, "ac", tr_ev)
            for part in range(2):
                for t in range(NT):
                    bi = t % 2
                    c0 = t * TT
                    hb = h2t[bi]
                    hc = self.chunks(hb)
                    p.wait(Y, "pe", h_free[bi])
                    hl = p.dma(Y, f"q_gh{bi}", hb[:, :].rearrange("p (c t) -> p c t", c=NCH),
                               self.fm(self.h2d, c0))
                    p.waitt(TEN, hl)
                    for oc in range(8):
                        pq = self.bank(oc % 2)
                        p.wait(TEN, "ac", ev_free.get(("sq", oc % 2), 0))
                        for dc in range(8):
                            slot = part * 64 + oc * 8 + dc
                            mm = TEN.matmul(pq, wT[:, slot * 128:(slot + 1) * 128], hc[dc],
                                            start=(dc == 0), stop=(dc == 7))
                        peq = p.inc(mm, "pe")
                        h_free[bi] = peq
                        p.wait(A, "pe", peq)
                        p.waitt(A, bq)
                        qk = qkt[:, (oc % 2) * 512:(oc % 2) * 512 + 512]
                        p.waitt(A, ev_free.get(("qkw", oc % 2), ("pe", 0)))
                        p.wait(A, "pe", ev_free.get(("sqm", oc % 2), 0))
                        ev = p.inc(A.activation(out=qk, in_=pq, func=AF.Identity,
                                                bias=bqk[:, part * 8 + oc:part * 8 + oc + 1]), "ac")
                        p.wait(A, "ac", ev)
                        sq = sqn[:, (oc % 2) * 512:(oc % 2) * 512 + 512]
                        ev2 = p.inc(A.activation(out=sq, in_=qk, func=AF.Square), "ac")
                        ev_free[("sq", oc % 2)] = ev2
                        p.wait(TEN, "ac", ev2)
                        p.wait(TEN, "dv", n_free.get("srow", 0))
                        mm = TEN.matmul(s1, self.ones128b, sq, start=(oc == 0), stop=(oc == 7))
                        ev_free[("sqm", oc % 2)] = p.inc(mm, "pe")
                        p.wait(SC, "ac", ev)
                        p.wait(V, "ac", ev)
                        if part == 0:
                            d = p.inc(V.tensor_copy(out=q_all(oc)[:, c0:c0 + TT], in_=qk), "dv")
                            ev_free[("qkw", oc % 2)] = ("dv", d)
                        else:
                            ev_free[("qkw", oc % 2)] = p.dma(
                                SC, f"q_gkw{oc % 2}",
                                self.kgin[t * 4:t * 4 + 4, oc].rearrange("j p x -> p j x"),
                                qk.rearrange("p (j x) -> p j x", j=4))
                    pes = ev_free[("sqm", 7 % 2)]
                    p.wait(V, "pe", pes)
                    p.waitt(V, n_free.get(("nw", t % 2), ("pe", 0)))
                    d1 = p.inc(V.tensor_copy(out=nslot(t), in_=s1), "dv")
                    n_free["srow"] = d1
                    if part == 0:
                        p.wait(V, "dv", d1)
                        d2 = p.inc(V.reduce_max(out=tmax, in_=nslot(t),
                                                axis=mybir.AxisListType.X), "dv")
                        p.wait(V, "dv", d2)
                        if t == 0:
                            d3 = V.tensor_copy(out=qm2, in_=tmax)
                        else:
                            d3 = V.tensor_tensor(out=qm2, in0=qm2, in1=tmax, op=OP.max)
                        n_free["srow"] = p.inc(d3, "dv")
                    else:
                        p.wait(Y, "dv", d1)
                        n_free[("nw", t % 2)] = p.dma(Y, f"q_gnw{t % 2}",
                                                      self.kn2d[c0:c0 + TT], nslot(t))
                if part == 1:
                    for par in range(2):
                        p.waitt(G, ev_free[("qkw", par)])
                        p.waitt(G, n_free.get(("nw", par), ("pe", 0)))
                    p.wait(G, "cc")
                    p.inc(G.collective_compute(
                        "AllGather", OP.bypass, replica_groups=RG,
                        ins=[self.kn2d], outs=[self.kn2f]), "cc")
            p.wait(V, "dv")
            p.wait(G, "dv")
            qmw = p.gdma("g_qm", self.qm2d, qm2)
            p.waitt(G, qmw)
            ccq = p.inc(G.collective_compute("AllReduce", OP.max, replica_groups=RG,
                                             ins=[self.qm2d], outs=[self.qm2f]), "cc")
            p.wait(G, "cc", ccq)
            qmr = p.gdma("g_qm", qm2, self.qm2f)
            self._ga_ccq = qmr
            self._ga_cck = p.inc(G.collective_compute(
                "AllGather", OP.bypass, replica_groups=RG,
                ins=[self.kgin], outs=[self.kgf]), "cck")
            self.fence(sems=[nm for nm in p.cnt if nm != "cck"])
        with contextlib.ExitStack() as ph:
            ent = ph.enter_context
            wvT = ent(nc.sbuf_tensor(self.uname("ga_wv"), [128, 8 * 1024], BF16))
            h2t = [ent(nc.sbuf_tensor(self.uname("ga_h2"), [128, NCH * TT], BF16))
                   for i in range(2)]
            vst = ent(nc.sbuf_tensor(self.uname("ga_vst"), [128, 2048], BF16))
            bvr = ent(nc.sbuf_tensor(self.uname("ga_bvr"), [1, 1024], BF16))
            pt = self.ps[3][:, 512:768].bitcast(BF16)
            tr_ev = 0
            for fc in range(8):
                wl = p.gdma(f"g_gw{fc % 2}", vst[:, (fc % 2) * 1024:(fc % 2) * 1024 + 1024],
                            self.inp["attn_W"][2048 + fc * 128:2048 + fc * 128 + 128, :])
                p.waitt(TEN, wl)
                for dc in range(8):
                    p.wait(TEN, "ac", tr_ev)
                    mm = TEN.transpose(pt[:, 0:128],
                                       vst[:, (fc % 2) * 1024 + dc * 128:
                                           (fc % 2) * 1024 + dc * 128 + 128], self.ident)
                    pet = p.inc(mm, "pe")
                    p.wait(A, "pe", pet)
                    tr_ev = p.inc(A.activation(out=wvT[:, (dc * 8 + fc) * 128:
                                                       (dc * 8 + fc) * 128 + 128],
                                               in_=pt[:, 0:128], func=AF.Copy), "ac")
                p.wait(G, "pe", p.cnt["pe"])
            bvl = p.gdma("g_gbv", bvr[:, :], self.inp["attn_b"][0:1, 2048:3072])
            ev_free = {}
            h_free = {0: 0, 1: 0}
            p.wait(TEN, "ac", tr_ev)
            for t in range(NT):
                bi = t % 2
                c0 = t * TT
                hb = h2t[bi]
                p.wait(Y, "pe", h_free[bi])
                hl = p.dma(Y, f"q_gh{bi}", hb[:, :].rearrange("p (c t) -> p c t", c=NCH),
                           self.fm(self.h2d, c0))
                p.waitt(TEN, hl)
                for sub in range(4):
                    for half in range(2):
                        pv = self.bank(sub % 2 * 2 + half)
                        p.wait(TEN, "ac", ev_free.get(("v", sub % 2, half), 0))
                        for dc in range(8):
                            TEN.matmul(pv, hb[:, dc * TT + sub * 128: dc * TT + sub * 128 + 128],
                                       wvT[:, dc * 1024 + half * 512:dc * 1024 + half * 512 + 512],
                                       start=(dc == 0), stop=False)
                        p.waitt(TEN, bvl)
                        mm = TEN.matmul(pv, self.onesb[0:1, 0:128],
                                        bvr[:, half * 512:(half + 1) * 512],
                                        start=False, stop=True)
                        pev = p.inc(mm, "pe")
                        h_free[bi] = pev
                        p.wait(A, "pe", pev)
                        vs = vst[:, (sub % 2) * 1024 + half * 512:(sub % 2) * 1024 + half * 512 + 512]
                        p.waitt(A, ev_free.get(("vw", sub % 2), ("pe", 0)))
                        ev = p.inc(A.activation(out=vs, in_=pv, func=AF.Copy), "ac")
                        ev_free[("v", sub % 2, half)] = ev
                    p.wait(SC, "ac", ev)
                    jc = t * 4 + sub
                    ev_free[("vw", sub % 2)] = p.dma(
                        SC, f"q_gvw{sub % 2}", self.vgin[jc],
                        vst[:, (sub % 2) * 1024:(sub % 2) * 1024 + 1024])
            for par in range(2):
                p.waitt(G, ev_free[("vw", par)])
            p.wait(G, "cc")
            ccv = p.inc(G.collective_compute("AllGather", OP.bypass, replica_groups=RG,
                                             ins=[self.vgin], outs=[self.vgf]), "ccv")
            self._ga_ccv = ccv
            self.fence(sems=[nm for nm in p.cnt if nm != "cck"])
        with contextlib.ExitStack() as ph:
            ent = ph.enter_context
            kstg = [ent(nc.sbuf_tensor(self.uname("ga_kst"), [128, 1024], BF16)) for i in range(2)]
            ebuf = ent(nc.sbuf_tensor(self.uname("ga_eb"), [128, 4 * 512], BF16))
            est = [ent(nc.sbuf_tensor(self.uname("ga_es"), [128, 512], BF16)) for i in range(2)]
            obuf = ent(nc.sbuf_tensor(self.uname("ga_ob"), [128, NCH * TT], F32))
            ncol = ent(nc.sbuf_tensor(self.uname("ga_nc"), [128, 64 + 2], F32))
            zall = ent(nc.sbuf_tensor(self.uname("ga_z"), [128, 64 + 8], F32))
            st2 = ent(nc.sbuf_tensor(self.uname("ga_st2"), [1, 2], BF16))
            p.waitt(V, self._ga_ccq)
            qcol = ncol[:, 64:65]
            self.bcast_col(qm2, qcol, st2[0:1, 0:2])
            p.wait(G, "cc")
            with nc.allow_non_contiguous_dma(reason="one-time negC col load"):
                kl = p.gdma("g_kn2", ncol[:, 0:32],
                            self.kn2f[0].rearrange("(jc p) -> p jc", p=128))
                kl = p.gdma("g_kn2b", ncol[:, 32:64],
                            self.kn2f[1].rearrange("(jc p) -> p jc", p=128))
            p.waitt(V, kl)
            p.wait(V, "g_kn2", p.cnt["g_kn2"])
            p.wait(V, "ac")
            d1 = p.inc(V.tensor_scalar(out=ncol[:, 0:64], in0=ncol[:, 0:64], scalar1=qcol,
                                       scalar2=None, op0=OP.mult), "dv")
            p.wait(A, "dv", d1)
            a1 = p.inc(A.activation(out=ncol[:, 0:64], in_=ncol[:, 0:64], func=AF.Ln), "ac")
            p.wait(A, "ac", a1)
            a2 = p.inc(A.activation(out=ncol[:, 0:64], in_=ncol[:, 0:64], func=AF.Exp,
                                    scale=0.5, bias=-math.log(1024.0)), "ac")
            p.wait(V, "ac", a2)
            d2 = p.inc(V.tensor_scalar(out=ncol[:, 0:64], in0=ncol[:, 0:64], scalar1=-1.0,
                                       scalar2=None, op0=OP.mult), "dv")
            negc_dv = d2
            p.wait(Y, "cck", self._ga_cck)
            kld = {}
            e_free = {}
            z_free = 0
            ew = {}
            for jc in range(64):
                bi = jc % 2
                p.wait(Y, "pe", kld.get(("used", bi), 0))
                kl = p.dma(Y, f"q_gk{bi}", kstg[bi][:, :].rearrange("p (dc x) -> p dc x", dc=8),
                           self.kgf[jc // 32, jc % 32].rearrange("dc p x -> p dc x"))
                for it in range(NT):
                    ps = self.bank(it % 2)
                    p.waitt(TEN, kl)
                    p.wait(TEN, "ac", e_free.get(it % 2, 0))
                    for dc in range(8):
                        mm = TEN.matmul(ps, kstg[bi][:, dc * 128:(dc + 1) * 128],
                                        q_all(dc)[:, it * TT:(it + 1) * TT],
                                        start=(dc == 0), stop=(dc == 7))
                    peS = p.inc(mm, "pe")
                    kld[("used", bi)] = peS
                    p.wait(A, "pe", peS)
                    p.wait(A, "dv", negc_dv)
                    ei = (jc * NT + it) % 4
                    eb = ebuf[:, ei * 512:ei * 512 + 512]
                    p.waitt(A, ew.get(ei, ("pe", 0)))
                    p.wait(A, "dv", z_free)
                    ea = p.inc(A.activation(out=eb, in_=ps, func=AF.Exp, scale=1.0 / 1024.0,
                                            bias=ncol[:, jc:jc + 1],
                                            accum_out=zall[:, 64 + it:64 + it + 1]), "ac")
                    e_free[it % 2] = ea
                    p.wait(SC, "ac", ea)
                    ew[ei] = p.dma(SC, f"q_gew{ei}", self.ed[jc, it], eb)
                p.wait(V, "ac", e_free[0])
                p.wait(V, "ac", e_free[1])
                z_free = p.inc(V.tensor_reduce(out=zall[:, jc:jc + 1], in_=zall[:, 64:72],
                                               axis=mybir.AxisListType.X, op=OP.add), "dv")
            p.wait(Y, "dv", z_free)
            zw = p.dma(Y, "q_gzw", self.zd, zall[:, 0:64])
            p.waitt(G, zw)
            p.wait(G, "cc")
            ccz = p.inc(G.collective_compute("AllReduce", OP.add, replica_groups=RG,
                                             ins=[self.zd], outs=[self.ztd]), "cc")
            p.wait(G, "cc", ccz)
            zr = p.gdma("g_gzr", zall[:, 0:64], self.ztd)
            p.waitt(A, zr)
            a1 = p.inc(A.activation(out=zall[:, 0:64], in_=zall[:, 0:64], func=AF.Ln), "ac")
            p.wait(A, "ac", a1)
            a2 = p.inc(A.activation(out=zall[:, 0:64], in_=zall[:, 0:64], func=AF.Exp,
                                    scale=-1.0), "ac")
            p.wait(V, "ccv", self._ga_ccv)
            p.wait(G, "ccv", self._ga_ccv)
            vl_free = {}
            stage_dv = {}
            p.wait(G, "pe")
            for jc in range(64):
                bi = jc % 2
                p.waitt(G, vl_free.get(bi, ("pe", 0)))
                vl = p.gdma(f"g_gvr{bi}", kstg[bi][:, :], self.vgf[jc // 32, jc % 32])
                p.waitt(V, vl)
                p.wait(V, "ac", a2)
                d = p.inc(V.tensor_scalar(
                    out=vres(jc), in0=kstg[bi][:, :], scalar1=zall[:, jc:jc + 1],
                    scalar2=None, op0=OP.mult), "dv")
                vl_free[bi] = ("dv", d)
                stage_dv[jc] = d
            for ei in range(4):
                if ei in ew:
                    p.waitt(Y, ew[ei])
            el_used = {0: 0, 1: 0}
            ob_w = ("pe", 0)
            for itb in range(NT):
                for jc in range(64):
                    bi = jc % 2
                    p.wait(Y, "pe", el_used[bi])
                    el = p.dma(Y, f"q_ge{bi}", est[bi][:, :], self.ed[jc, itb])
                    p.waitt(TEN, el)
                    if jc == 0:
                        p.waitt(TEN, ob_w)
                        p.wait(TEN, "ac", a2)
                    if itb == 0:
                        p.wait(TEN, "dv", stage_dv[jc])
                    for dvc in range(8):
                        mm = TEN.matmul(self.bank(dvc), vres(jc)[:, dvc * 128:(dvc + 1) * 128],
                                        est[bi][:, :], start=(jc == 0), stop=(jc == 63))
                    el_used[bi] = p.inc(mm, "pe")
                pe2 = el_used[(64 - 1) % 2]
                p.wait(A, "pe", pe2)
                last = None
                for dvc in range(8):
                    last = A.activation(out=obuf[:, dvc * TT:(dvc + 1) * TT],
                                        in_=self.bank(dvc), func=AF.Copy)
                aev = p.inc(last, "ac")
                p.wait(Y, "ac", aev)
                ob_w = p.dma(Y, "q_gow", self.fm(self.aod, itb * TT),
                             obuf[:, :].rearrange("p (c t) -> p c t", c=NCH))
            self.fence()

    def mask(self, cqc, cr):
        i = cqc * 5 + cr
        return self.masksb[:, i * 128:(i + 1) * 128]

    def emit_layer_prep(self, l, st):
        nc, p = self.nc, self.p
        TEN, V, A, Y = nc.tensor, nc.vector, nc.scalar, nc.sync
        r = self.smalls
        qsr, ksr = r[0:1, 64:192], r[0:1, 192:320]
        sc = r[0:1, 320:326]
        p.wait(Y, "dv")
        p.dma(Y, "q_prep", qsr, self.inp["qscale"][l:l + 1, :])
        ld = p.dma(Y, "q_prep", ksr, self.inp["kscale"][l:l + 1, :])
        p.waitt(V, ld)
        p.wait(V, "dv")
        p.inc(V.reduce_max(out=sc[:, 0:1], in_=qsr, axis=mybir.AxisListType.X,
                           apply_absolute_value=True), "dv")
        p.wait(V, "dv")
        p.inc(V.reduce_max(out=sc[:, 1:2], in_=ksr, axis=mybir.AxisListType.X,
                           apply_absolute_value=True), "dv")
        p.wait(V, "dv")
        p.inc(V.tensor_tensor(out=sc[:, 2:3], in0=sc[:, 0:1], in1=sc[:, 1:2], op=OP.mult), "dv")
        p.wait(V, "dv")
        p.inc(V.memset(self.cbf[:, 258:260], 0.0), "dv")
        p.wait(V, "dv")
        p.inc(V.tensor_copy(out=self.cbf[0:1, 258:259], in_=sc[:, 2:3]), "dv")
        p.wait(TEN, "dv")
        p.waitt(TEN, self.consts_ld)
        p.wait(TEN, "ac")
        mmc = TEN.matmul(self.bank(4)[:, 0:2], self.onesb[0:1, 0:128],
                         self.cbf[0:1, 258:260], start=True, stop=True)
        mmq = TEN.matmul(self.ps[3][0:8, 0:128], self.onesrow[0:1, 0:8], qsr,
                         start=True, stop=True)
        mmk = TEN.matmul(self.ps[3][0:8, 512:640], self.onesrow[0:1, 0:8], ksr,
                         start=True, stop=True)
        peq = p.inc(mmk, "pe")
        p.wait(A, "pe", peq)
        p.inc(A.activation(out=self.negC, in_=self.bank(4)[:, 0:1], func=AF.Copy,
                           scale=-16.0), "ac")
        p.wait(V, "pe", peq)
        last = None
        for c in range(4):
            last = V.tensor_tensor(out=st["indqs"][0:8, c * 128:(c + 1) * 128],
                                   in0=self.ind8s[:, c * 128:(c + 1) * 128],
                                   in1=self.ps[3][0:8, 0:128], op=OP.mult)
            last = V.tensor_tensor(out=st["indks"][0:8, c * 128:(c + 1) * 128],
                                   in0=self.ind8s[:, c * 128:(c + 1) * 128],
                                   in1=self.ps[3][0:8, 512:640], op=OP.mult)
        p.inc(last, "dv")

    def tap_dram(self, name, src, shape, dtype):
        nc, p = self.nc, self.p
        d = nc.dram_tensor(name, shape, dtype, kind="ExternalOutput").ap()
        self.extra_outputs.append(name)
        self.fence(engines=[nc.sync])
        p.dma(nc.sync, "q_tap", d, src)
        return d

    def tap_fm(self, name, sbuf_src_fn):
        nc, p = self.nc, self.p
        d = nc.dram_tensor(name, [D, T], F32, kind="ExternalOutput").ap()
        self.extra_outputs.append(name)
        self.fence(engines=[nc.sync])
        p.dma(nc.sync, "q_tap", d.rearrange("(c p) t -> p c t", p=128), sbuf_src_fn())
        return d

    def build(self):
        nc = self.nc
        self.declare()
        with contextlib.ExitStack() as ctx:
            self.alloc(ctx)
            p = self.p
            with nc.Block() as block:
                @block.gpsimd
                def _(g):
                    self.load_consts()
                    self.emit_boundary(self.inp["x_fm"], add_src=None, write_x=None, make="y")
                    self.emit_ycol_exchange()
                    if "y1" in self.taps:
                        self.tap_fm("dbg_y1", lambda: self.y[:, :]
                                    .rearrange("p (c t) -> p c t", c=NCH)[:, :, 1:YST])
                    for l in range(2):
                        if self.upto < 2 + 3 * l:
                            break
                        self.emit_ffn_qkv(l)
                        if l == 0 and "qkv" in self.taps:
                            self.tap_dram("dbg_q", self.qd, [4, 128, T], BF16)
                            self.tap_dram("dbg_k", self.kd, [32, 4, 128, 128], BF16)
                            self.tap_dram("dbg_v", self.vd, [32, 128, INNER], BF16)
                            self.tap_dram("dbg_kh", self.khalo, [4, 4, 128, 128], BF16)
                            self.tap_dram("dbg_vh", self.vhalo, [4, 128, INNER], BF16)
                        if self.upto < 3 + 3 * l:
                            break
                        self.emit_ffn_attn(l)
                        if l == 0 and "yw" in self.taps:
                            self.tap_fm("dbg_yw", lambda: self.y[:, :]
                                        .rearrange("p (c t) -> p c t", c=NCH)[:, :, 1:YST])
                        if self.upto < 4 + 3 * l:
                            break
                        self.emit_ffn_mlp(l)
                        self.emit_ycol_exchange()
                        if l == 0 and "yl0" in self.taps:
                            self.tap_fm("dbg_yl0", lambda: self.y[:, :]
                                        .rearrange("p (c t) -> p c t", c=NCH)[:, :, 1:YST])
                    if "yc1" in self.taps:
                        self.tap_fm("dbg_yc1", lambda: self.y[:, :]
                                    .rearrange("p (c t) -> p c t", c=NCH)[:, :, 1:YST])
                    if self.upto >= 8:
                        self.emit_boundary(self.inp["x_fm"], add_src="y",
                                           write_x=self.x2, make="h2")
                    if self.upto >= 9:
                        self.emit_global_attn()
                        if "ga" in self.taps:
                            self.tap_dram("dbg_gao", self.aod, [D, T], F32)
                    if self.upto >= 10:
                        self.emit_boundary(self.x2, add_src=self.aod,
                                           write_x=self.x3, make="y")
                        self.emit_ycol_exchange()
                    for l in range(2):
                        if self.upto < 11 + 3 * l:
                            break
                        self.emit_ffn_qkv(l)
                        if self.upto < 12 + 3 * l:
                            break
                        self.emit_ffn_attn(l)
                        if self.upto < 13 + 3 * l:
                            break
                        self.emit_ffn_mlp(l)
                        self.emit_ycol_exchange()
                    if self.upto >= 17:
                        self.emit_boundary(self.x3, add_src="y",
                                           write_x=self.out_fm, make=None)
                    if self.upto < 17:
                        p.wait(nc.sync, "dv")
                        p.dma(nc.sync, "q_out", self.out_fm.rearrange("(c p) t -> p c t", p=128),
                              self.y[:, :].rearrange("p (c t) -> p c t", c=NCH)[:, :, 1:YST])
                    for nm, c in sorted(p.cnt.items()):
                        if nm.startswith(("q", "g_")) and c:
                            nc.sync.wait_ge(p.sems[nm], c)
        return nc


def build_nc(debug_taps=(), upto=99):
    mk = MK(debug_taps=debug_taps)
    mk.upto = upto
    return mk.build()


def kernel(**inputs):
    consts = [host_constants(c) for c in range(8)]
    nc = build_nc()
    in_maps = [make_in_map(inputs, c, consts) for c in range(8)]
    res = run_bass_kernel_spmd(nc, in_maps, core_ids=list(range(8)))
    out = np.empty((B, S, D), np.float32)
    for c, r in enumerate(res.results):
        b, h = c // 2, c % 2
        out[b, h * T:(h + 1) * T, :] = np.asarray(r["out_fm"]).T
    return out


# version 14
# speedup vs baseline: 1.0444x; 1.0444x over previous
import contextlib
import math
import numpy as np
import ml_dtypes
import concourse.bass as bass
import concourse.mybir as mybir
from concourse.bass_utils import run_bass_kernel_spmd

F32 = mybir.dt.float32
BF16 = mybir.dt.bfloat16
AF = mybir.ActivationFunctionType
OP = mybir.AluOpType

B, S, D = 4, 8192, 1024
T = 4096
NCH = D // 128
TT = 512
NT = T // TT
HEADS, DH, WIN, INNER, HID = 8, 64, 512, 512, 2730
YST = T + 1
LN05 = math.log(0.5)
RG = [[0, 1], [2, 3], [4, 5], [6, 7]]

BF = ml_dtypes.bfloat16


def host_constants(core_id):
    h = core_id % 2
    t0g = h * T
    c = {}
    c["ident"] = np.eye(128, dtype=BF)
    c["ones128b"] = np.ones((128, 1), dtype=BF)
    P = np.zeros((128, 128), dtype=np.float32)
    for hh in range(2):
        o = hh * 64
        for d in range(32):
            P[o + d, o + d + 32] = -1.0
            P[o + d + 32, o + d] = 1.0
    c["PT"] = P.T.astype(BF)
    c["PTf"] = P.T.astype(np.float32)
    half_idx = np.arange(0, DH, 2, dtype=np.float64)
    inv_freq = 1.0 / 10000.0 ** (half_idx / DH)
    t = np.arange(t0g, t0g + T, dtype=np.float64)
    freqs = t[:, None] * inv_freq[None, :]
    freqs = np.concatenate([freqs, freqs], axis=-1)
    base = (half_idx + 0.4 * DH) / (1.4 * DH)
    power = (t - S // 2) / float(WIN // 2)
    scale = base[None, :] ** power[:, None]
    scale = np.concatenate([scale, scale], axis=-1)
    cos, sin = np.cos(freqs), np.sin(freqs)
    dup = lambda a: np.concatenate([a, a], axis=0).astype(np.float32)
    c["csq"] = dup((cos * scale * 8.0).T)
    c["ssq"] = dup((sin * scale * 8.0).T)
    c["csk"] = dup((cos / scale).T)
    c["ssk"] = dup((sin / scale).T)
    ii = np.arange(128)[None, :]
    pp = np.arange(128)[:, None]
    masks = np.zeros((5, 5, 128, 128), dtype=BF)
    for cqc in range(5):
        for cr in range(5):
            off = 128 * (cr - 4)
            valid = (off + pp - ii >= -512) & (off + pp - ii <= 0)
            if cqc < 4 and h == 0 and (cqc - 4 + cr) < 0:
                valid = np.zeros_like(valid)
            masks[cqc, cr] = valid.astype(BF)
    c["masks"] = masks
    c["wgate"] = np.full((128, 1), 1.0 if h == 0 else 0.0, dtype=np.float32)
    c["rgate"] = np.full((128, 1), 0.0 if h == 0 else 1.0, dtype=np.float32)
    ind = np.zeros((4, 128, 8), dtype=np.float32)
    for cc in range(4):
        ind[cc, 0:64, 2 * cc] = 1.0
        ind[cc, 64:128, 2 * cc + 1] = 1.0
    c["ind8T"] = ind
    ind2 = np.zeros((4, 8, 128), dtype=np.float32)
    for cc in range(4):
        ind2[cc, 2 * cc, 0:64] = 1.0
        ind2[cc, 2 * cc + 1, 64:128] = 1.0
    c["ind8"] = ind2
    return c


def make_in_map(inputs, core_id, consts):
    b, h = core_id // 2, core_id % 2
    x = np.asarray(inputs["L0_attn_x"])[b, h * T:(h + 1) * T, :]
    m = {
        "x_fm": np.ascontiguousarray(x.T.astype(np.float32)),
        "attn_W": np.asarray(inputs["L0_attn_W"], dtype=np.float32),
        "attn_b": np.asarray(inputs["L0_attn_b"], dtype=np.float32).reshape(1, 3 * D),
        "wqkv": np.asarray(inputs["L0_ffn_wqkv"], dtype=np.float32),
        "qscale": np.tile(np.asarray(inputs["L0_ffn_q_scale"], dtype=np.float32), (1, 2)),
        "kscale": np.tile(np.asarray(inputs["L0_ffn_k_scale"], dtype=np.float32), (1, 2)),
        "wo": np.asarray(inputs["L0_ffn_wo"], dtype=np.float32),
        "ln_g": np.asarray(inputs["L0_ffn_ln_g"], dtype=np.float32),
        "ln_b": np.asarray(inputs["L0_ffn_ln_b"], dtype=np.float32),
        "w1": np.asarray(inputs["L0_ffn_w1"], dtype=np.float32),
        "b1": np.asarray(inputs["L0_ffn_b1"], dtype=np.float32),
        "w2": np.asarray(inputs["L0_ffn_w2"], dtype=np.float32),
        "b2": np.asarray(inputs["L0_ffn_b2"], dtype=np.float32),
    }
    m.update(consts[core_id])
    return m


class Prog:

    def __init__(self, nc, ctx):
        self.nc = nc
        self.ctx = ctx
        self.sems = {}
        self.cnt = {}
        self.last_wait = {}
        for n in ("pe", "dv", "ac", "po", "cc"):
            self.mksem(n)

    def mksem(self, n):
        if n not in self.sems:
            self.sems[n] = self.ctx.enter_context(self.nc.semaphore(n))
            self.cnt[n] = 0
        return self.sems[n]

    def inc(self, instr, name, amt=1):
        instr.then_inc(self.sems[name], amt)
        self.cnt[name] += amt
        return self.cnt[name]

    def dma(self, eng, tag, out, in_, **kw):
        self.mksem(tag)
        i = eng.dma_start(out=out, in_=in_, **kw)
        return tag, self.inc(i, tag, 16)

    def gdma(self, tag, out, in_, **kw):
        return self.dma(self.nc.gpsimd, tag, out, in_, **kw)

    def waitt(self, eng, tok):
        self.wait(eng, tok[0], tok[1])

    def wait(self, eng, name, val=None):
        v = self.cnt[name] if val is None else val
        if v <= 0:
            return
        key = (eng.engine, name)
        if self.last_wait.get(key, -1) >= v:
            return
        eng.wait_ge(self.sems[name], v)
        self.last_wait[key] = v


class MK:

    def __init__(self, debug_taps=()):
        self.taps = set(debug_taps)
        self.upto = 99
        self.nc = bass.Bass("TRN2", target_bir_lowering=False,
                            dynamic_dma_scratch_size=16384)
        self.extra_outputs = []
        self._uid = 0

    def uname(self, n):
        self._uid += 1
        return f"{n}_{self._uid}"

    def declare(self):
        nc = self.nc
        specs = [
            ("x_fm", [D, T], F32), ("attn_W", [3 * D, D], F32), ("attn_b", [1, 3 * D], F32),
            ("wqkv", [2, D, 3 * INNER], F32), ("qscale", [2, 128], F32), ("kscale", [2, 128], F32),
            ("wo", [2, INNER, D], F32), ("ln_g", [2, D], F32), ("ln_b", [2, D], F32),
            ("w1", [2, D, 2 * HID], F32), ("b1", [2, 2 * HID], F32),
            ("w2", [2, HID, D], F32), ("b2", [2, D], F32),
            ("ident", [128, 128], BF16), ("ones128b", [128, 1], BF16),
            ("PT", [128, 128], BF16), ("PTf", [128, 128], F32),
            ("csq", [128, T], F32), ("ssq", [128, T], F32),
            ("csk", [128, T], F32), ("ssk", [128, T], F32),
            ("masks", [5, 5, 128, 128], BF16),
            ("wgate", [128, 1], F32), ("rgate", [128, 1], F32),
            ("ind8T", [4, 128, 8], F32), ("ind8", [4, 8, 128], F32),
        ]
        self.inp = {n: nc.dram_tensor(n, sh, dt, kind="ExternalInput").ap() for n, sh, dt in specs}
        self.out_fm = nc.dram_tensor("out_fm", [D, T], F32, kind="ExternalOutput").ap()
        self.x2 = nc.dram_tensor("x2_hbm", [D, T], F32).ap()
        self.x3 = nc.dram_tensor("x3_hbm", [D, T], F32).ap()
        self.h2d = nc.dram_tensor("h2_hbm", [D, T], BF16).ap()
        self.ycol_in = nc.dram_tensor("ycol_in", [128, NCH], F32).ap()
        self.ycol_out = nc.dram_tensor("ycol_out", [128, NCH], F32).ap()
        self.qd = nc.dram_tensor("q_hbm", [4, 128, T], BF16).ap()
        self.kd = nc.dram_tensor("k_hbm", [32, 4, 128, 128], BF16).ap()
        self.vd = nc.dram_tensor("v_hbm", [32, 128, INNER], BF16).ap()
        self.khalo_in = nc.dram_tensor("khalo_in", [4, 4, 128, 128], BF16).ap()
        self.khalo = nc.dram_tensor("khalo", [4, 4, 128, 128], BF16).ap()
        self.vhalo_in = nc.dram_tensor("vhalo_in", [4, 128, INNER], BF16).ap()
        self.vhalo = nc.dram_tensor("vhalo", [4, 128, INNER], BF16).ap()
        self.zrowd = nc.dram_tensor("zrow_d", [8, 512], F32).ap()
        self.gbd = nc.dram_tensor("gb_d", [2, D], BF16).ap()
        self.h1d = nc.dram_tensor("h1_hbm", [D, T], BF16).ap()
        self.aod = nc.dram_tensor("ao_hbm", [D, T], F32).ap()
        self.kgin = nc.dram_tensor("kgin", [32, 8, 128, 128], BF16).ap()
        self.vgin = nc.dram_tensor("vgin", [32, 128, 1024], BF16).ap()
        self.kgf = nc.dram_tensor("kgf", [2, 32, 8, 128, 128], BF16).ap()
        self.vgf = nc.dram_tensor("vgf", [2, 32, 128, 1024], BF16).ap()
        self.kn2d = nc.dram_tensor("kn2d", [T], F32).ap()
        self.kn2f = nc.dram_tensor("kn2f", [2, T], F32).ap()
        self.qm2d = nc.dram_tensor("qm2d", [1, 1], F32).ap()
        self.qm2f = nc.dram_tensor("qm2f", [1, 1], F32).ap()
        self.ed = nc.dram_tensor("e_hbm", [64, 8, 128, 512], BF16).ap()
        self.zd = nc.dram_tensor("zd", [128, 64], F32).ap()
        self.ztd = nc.dram_tensor("ztd", [128, 64], F32).ap()
    def alloc(self, ctx):
        nc = self.nc
        ent = ctx.enter_context
        self.y = ent(nc.sbuf_tensor("y", [128, NCH * YST], F32))
        self.smalls = ent(nc.sbuf_tensor("smalls", [128, 2112], F32))
        self.colsc = self.smalls[:, 0:64]
        self.row = self.smalls[0:1, 64:64 + 4 * TT]
        self.cpool = ent(nc.sbuf_tensor("cpool", [128, 8], F32))
        self.cpool_vals = {}
        self.cbf = ent(nc.sbuf_tensor("cbf", [128, 512], BF16))
        self.onesrow = ent(nc.sbuf_tensor("onesrow", [1, 128], F32))
        self.onesb = ent(nc.sbuf_tensor("onesb", [1, 512], BF16))
        self.ind8Ts = ent(nc.sbuf_tensor("ind8Ts", [128, 32], F32))
        self.ind8s = ent(nc.sbuf_tensor("ind8s", [8, 512], F32))
        self.PTf = ent(nc.sbuf_tensor("PTf_sb", [128, 128], F32))
        self.ps = [ent(nc.psum_tensor(f"ps{i}", [128, 1024], F32)) for i in range(4)]
        self.p = Prog(nc, ctx)
        self.negC = self.smalls[:, 2:3]

    def bank(self, i):
        return self.ps[i // 2][:, (i % 2) * 512:(i % 2) * 512 + 512]

    def load_consts(self):
        nc, p = self.nc, self.p
        Y = nc.sync
        p.dma(Y, "q_const", self.cbf[:, 0:128], self.inp["ident"])
        p.dma(Y, "q_const", self.cbf[:, 128:256], self.inp["PT"])
        p.dma(Y, "q_const", self.cbf[:, 256:257], self.inp["ones128b"])
        p.dma(Y, "q_const", self.smalls[:, 0:1], self.inp["wgate"])
        p.dma(Y, "q_const", self.smalls[:, 1:2], self.inp["rgate"])
        p.dma(Y, "q_const", self.PTf[:, :], self.inp["PTf"])
        p.dma(Y, "q_const", self.ind8Ts[:, :].rearrange("p (a b) -> p a b", a=4),
              self.inp["ind8T"].rearrange("a p b -> p a b"))
        p.dma(Y, "q_const", self.ind8s[:, :].rearrange("p (a b) -> p a b", a=4),
              self.inp["ind8"].rearrange("a p b -> p a b"))
        self.ident = self.cbf[:, 0:128]
        self.PT = self.cbf[:, 128:256]
        self.ones128b = self.cbf[:, 256:257]
        self.wgate = self.smalls[:, 0:1]
        self.rgate = self.smalls[:, 1:2]
        self.consts_ld = ("q_const", p.cnt["q_const"])
        p.inc(nc.vector.memset(self.onesrow[:, :], 1.0), "dv")
        p.inc(nc.vector.memset(self.onesb[:, :], 1.0), "dv")
        for v in (0.0, 1e-5, LN05, 1e-18, -math.log(1024.0)):
            self.constcol(v)
        self.cpool_done = p.cnt["dv"]

    def constcol(self, value):
        if value not in self.cpool_vals:
            i = len(self.cpool_vals)
            ap = self.cpool[:, i:i + 1]
            self.p.inc(self.nc.vector.memset(ap, float(value)), "dv")
            self.nc.const_aps.register(F32, float(value), ap)
            self.cpool_vals[value] = ap
        return self.cpool_vals[value]

    def chunks(self, buf, n=TT):
        return [buf[:, c * n:(c + 1) * n] for c in range(NCH)]

    def ytile(self, c, c0, n=TT, shift=0):
        return self.y[:, c * YST + 1 + c0 - shift: c * YST + 1 + c0 - shift + n]

    def fm(self, dram, c0, n=TT):
        return dram.rearrange("(c p) t -> p c t", p=128)[:, :, c0:c0 + n]

    def fence(self, engines=None, sems=None):
        nc, p = self.nc, self.p
        engines = engines or [nc.tensor, nc.vector, nc.scalar, nc.gpsimd, nc.sync]
        for eng in engines:
            for nm, c in p.cnt.items():
                if c and (sems is None or nm in sems):
                    p.wait(eng, nm, c)

    def emit_boundary(self, src, add_src, write_x, make):
        nc, p = self.nc, self.p
        TEN, V, A, G, Y = nc.tensor, nc.vector, nc.scalar, nc.gpsimd, nc.sync
        with contextlib.ExitStack() as ph:
            ent = ph.enter_context
            xt = [ent(nc.sbuf_tensor(self.uname(f"bxt{i}"), [128, NCH * TT], F32)) for i in range(2)]
            xb = [ent(nc.sbuf_tensor(self.uname(f"bxb{i}"), [128, NCH * TT], BF16)) for i in range(2)]
            at = None
            sqb = ent(nc.sbuf_tensor(self.uname("bsq"), [128, NCH * TT], BF16))
            s1s = [self.bank(0)[0:1, :], self.bank(4)[0:1, :]]
            s2s = [self.bank(1)[0:1, :], self.bank(5)[0:1, :]]
            rbs = [self.bank(2), self.bank(6)]
            sbs = [self.bank(3), self.bank(7)]
            bufdone = {0: 0, 1: 0}
            h2wd = {0: ("pe", 0), 1: ("pe", 0)}
            wxd = {0: ("pe", 0), 1: ("pe", 0)}
            p.wait(nc.scalar, "dv", self.cpool_done)
            pe_prev = {0: 0, 1: 0}
            dv_prev = {0: 0, 1: 0}
            ldt, ldat = {}, {}

            def issue_load(tl):
                bl = tl % 2
                xtl = xt[bl]
                p.wait(Y, "dv", bufdone[bl])
                p.waitt(Y, wxd[bl])
                ldt[tl] = p.dma(Y, f"q_xt{bl}", xtl[:, :].rearrange("p (c t) -> p c t", c=NCH),
                                self.fm(src, tl * TT))
                if add_src is not None and add_src != "y":
                    p.waitt(G, ldt[tl])
                    ldat[tl] = p.gdma(f"g_at{bl}", xtl[:, :].rearrange("p (c t) -> p c t", c=NCH),
                                      self.fm(add_src, tl * TT), accum_op=OP.add)
                elif add_src is None:
                    p.wait(G, "dv", bufdone[bl])
                    ldat[tl] = p.gdma(f"qg_xb{bl}", xb[bl][:, :].rearrange("p (c t) -> p c t", c=NCH),
                                      self.fm(src, tl * TT))

            issue_load(0)
            for t in range(NT):
                bi = t % 2
                xtb, xbb = xt[bi], xb[bi]
                xtc, xbc, sqc = self.chunks(xtb), self.chunks(xbb), self.chunks(sqb)
                s1, s2, rb, sb = s1s[bi], s2s[bi], rbs[bi], sbs[bi]
                c0 = t * TT
                if t + 1 < NT:
                    issue_load(t + 1)
                ld = ldt[t]
                wx = None
                if add_src is not None and add_src != "y":
                    lda = ldat[t]
                    p.waitt(V, lda)
                    p.waitt(Y, lda)
                    added = p.cnt["dv"]
                elif add_src == "y":
                    p.waitt(V, ld)
                    last = None
                    for c in range(NCH):
                        src2 = self.ytile(c, c0)
                        last = V.tensor_tensor(out=xtc[c], in0=xtc[c], in1=src2, op=OP.add)
                    added = p.inc(last, "dv")
                    p.wait(Y, "dv", added)
                if add_src is not None:
                    wx = p.dma(Y, f"q_xw{bi}", self.fm(write_x, c0),
                               xtb[:, :].rearrange("p (c t) -> p c t", c=NCH))
                    wxd[bi] = wx
                    if make is None:
                        bufdone[bi] = added
                        continue
                    p.wait(V, "dv", added)
                    p.waitt(V, h2wd[bi])
                    for c in range(NCH):
                        last = V.tensor_copy(out=xbc[c], in_=xtc[c])
                    bconv = p.inc(last, "dv")
                    p.wait(A, "dv", bconv)
                    p.wait(TEN, "dv", bconv)
                else:
                    gld = ldat[t]
                    p.waitt(A, gld)
                    p.waitt(TEN, gld)
                last = None
                for c in range(NCH):
                    last = A.activation(out=sqc[c], in_=xbc[c], func=AF.Square)
                acsq = p.inc(last, "ac")
                p.waitt(TEN, self.consts_ld)
                p.wait(TEN, "pe", pe_prev[bi])
                p.wait(TEN, "dv", dv_prev[bi])
                for c in range(NCH):
                    TEN.matmul(s1, self.ones128b, xbc[c], start=(c == 0), stop=(c == NCH - 1))
                p.wait(TEN, "ac", acsq)
                for c in range(NCH):
                    mm = TEN.matmul(s2, self.ones128b, sqc[c], start=(c == 0), stop=(c == NCH - 1))
                pe_stats = p.inc(mm, "pe")
                r = self.row
                mu, t1r, t2r = (r[:, i * TT:(i + 1) * TT] for i in range(3))
                p.wait(V, "pe", pe_stats)
                d1 = p.inc(V.tensor_scalar(out=mu, in0=s1, scalar1=1.0 / D, scalar2=None,
                                           op0=OP.mult), "dv")
                p.wait(V, "dv", d1)
                d2 = p.inc(V.tensor_tensor(out=t1r, in0=mu, in1=mu, op=OP.mult), "dv")
                p.wait(V, "dv", d2)
                d3 = p.inc(V.scalar_tensor_tensor(out=t1r, in0=s2, scalar=1.0 / D, in1=t1r,
                                                  op0=OP.mult, op1=OP.subtract), "dv")
                p.wait(A, "dv", d3)
                p.wait(A, "pe", pe_prev[1 - bi])
                a1 = p.inc(A.activation(out=t2r, in_=t1r, func=AF.Ln, bias=1e-5, scale=1.0), "ac")
                p.wait(A, "ac", a1)
                bias = LN05 if make == "y" else 0.0
                a2 = p.inc(A.activation(out=t2r, in_=t2r, func=AF.Exp, bias=bias, scale=-0.5), "ac")
                rp = t2r
                p.wait(V, "ac", a2)
                d4 = p.inc(V.tensor_tensor(out=t1r, in0=mu, in1=rp, op=OP.mult), "dv")
                sp = t1r
                p.wait(TEN, "ac", a2)
                p.wait(TEN, "dv", d4)
                TEN.matmul(rb, self.onesrow[:, :], rp, start=True, stop=True)
                mmb = TEN.matmul(sb, self.onesrow[:, :], sp, start=True, stop=True)
                pe_b = p.inc(mmb, "pe")
                pe_prev[bi] = pe_b
                p.wait(V, "pe", pe_b)
                p.waitt(V, ld)
                if wx is not None:
                    p.waitt(V, wx)
                for c in range(NCH):
                    last = V.tensor_tensor(out=xtc[c], in0=xtc[c], in1=rb, op=OP.mult)
                dm = p.inc(last, "dv")
                p.wait(V, "dv", dm)
                for c in range(NCH):
                    dst = self.ytile(c, c0) if make == "y" else xbc[c]
                    last = V.tensor_tensor(out=dst, in0=xtc[c], in1=sb, op=OP.subtract)
                vd = p.inc(last, "dv")
                dv_prev[bi] = vd
                if make == "h2":
                    p.wait(Y, "dv", vd)
                    h2wd[bi] = p.dma(Y, f"q_h2w{bi}", self.fm(self.h2d, c0),
                                     xbb[:, :].rearrange("p (c t) -> p c t", c=NCH))
                bufdone[bi] = vd
            self.fence()

    def emit_ycol_exchange(self):
        nc, p = self.nc, self.p
        V, G = nc.vector, nc.gpsimd
        yv = self.y[:, :].rearrange("p (c t) -> p c t", c=NCH)
        lastcols = yv[:, :, YST - 1]
        tmp = self.smalls[:, 8:8 + NCH]
        p.waitt(V, self.consts_ld)
        p.wait(V, "dv")
        d1 = p.inc(V.tensor_scalar(out=tmp, in0=lastcols, scalar1=self.wgate, scalar2=None,
                                   op0=OP.mult), "dv")
        p.wait(G, "dv", d1)
        g1 = p.gdma("qg_ycol", self.ycol_in, tmp)
        p.waitt(G, g1)
        cc = p.inc(G.collective_compute("AllReduce", OP.add, replica_groups=RG,
                                        ins=[self.ycol_in], outs=[self.ycol_out]), "cc")
        p.wait(G, "cc", cc)
        hcol = self.smalls[:, 8 + NCH:8 + 2 * NCH]
        g2 = p.gdma("qg_ycol", hcol, self.ycol_out)
        p.waitt(V, g2)
        d2 = p.inc(V.tensor_scalar(out=hcol, in0=hcol, scalar1=self.rgate, scalar2=None,
                                   op0=OP.mult), "dv")
        p.wait(V, "dv", d2)
        halocols = yv[:, :, 0]
        return p.inc(V.tensor_copy(out=halocols, in_=hcol), "dv")

    def emit_ffn_qkv(self, l):
        nc, p = self.nc, self.p
        TEN, V, A, G, Y, PO = nc.tensor, nc.vector, nc.scalar, nc.gpsimd, nc.sync, nc.gpsimd
        SC = nc.scalar
        with contextlib.ExitStack() as ph:
            ent = ph.enter_context
            wqk = ent(nc.sbuf_tensor(self.uname("wqk"), [128, 8192], BF16))
            wv = ent(nc.sbuf_tensor(self.uname("wv"), [128, 4096], BF16))
            h = ent(nc.sbuf_tensor(self.uname("hq"), [128, 2 * 4096], BF16))
            tb = ent(nc.sbuf_tensor(self.uname("tbq"), [128, 1024], F32))
            qraw = ent(nc.sbuf_tensor(self.uname("qraw"), [128, 2048], F32))
            qn = ent(nc.sbuf_tensor(self.uname("qn"), [128, 512], F32))
            t1b = ent(nc.sbuf_tensor(self.uname("t1b"), [128, 512], F32))
            qf = ent(nc.sbuf_tensor(self.uname("qf"), [128, 1024], BF16))
            vst = ent(nc.sbuf_tensor(self.uname("vstb"), [128, 512], BF16))
            gate = ent(nc.sbuf_tensor(self.uname("gateb"), [128, 512], BF16))
            sqn = ent(nc.sbuf_tensor(self.uname("sqn"), [128, 512], F32))
            nf = sqn
            nb = nf
            indqs = ent(nc.sbuf_tensor(self.uname("indqs"), [8, 512], F32))
            indks = ent(nc.sbuf_tensor(self.uname("indks"), [8, 512], F32))
            st = dict(indqs=indqs, indks=indks)
            self.emit_layer_prep(l, st)
            for oc in range(8):
                wq_ld = p.gdma("g_wqk", wqk[:, oc * 1024:(oc + 1) * 1024]
                               .rearrange("p (kc c) -> p kc c", kc=8),
                               self.inp["wqkv"][l][:, oc * 128:(oc + 1) * 128]
                               .rearrange("(kc p) c -> p kc c", p=128))
            for kc in range(8):
                wv_ld = p.gdma("g_wv", wv[:, kc * 512:(kc + 1) * 512],
                               self.inp["wqkv"][l][kc * 128:(kc + 1) * 128, 1024:1536])
            p.wait(nc.vector, "dv")
            h_free = {0: 0, 1: 0}
            tbl_free = [0, 0]
            ev_free = {}
            pb_free = 0
            qn_free = 0
            for t in range(NT):
                bi = t % 2
                c0 = t * TT
                hb = h[:, bi * 4096:bi * 4096 + 4096]
                hc = self.chunks(hb)
                p.wait(V, "pe", h_free[bi])
                last = None
                for c in range(NCH):
                    yt = self.ytile(c, c0) if c < 4 else self.ytile(c, c0, shift=1)
                    last = V.tensor_scalar(out=hc[c], in0=yt, scalar1=2.0, scalar2=None,
                                           op0=OP.mult)
                hdone = p.inc(last, "dv")
                tbs = [tb[:, 0:TT], tb[:, TT:2 * TT]]
                p.waitt(TEN, wv_ld)
                p.wait(TEN, "dv", hdone)
                for sub in range(4):
                    pv = self.bank(6 + sub % 2)
                    p.wait(TEN, "ac", ev_free.get(("v", sub % 2), 0))
                    for kc in range(8):
                        mm = TEN.matmul(pv, hb[:, kc * TT + sub * 128: kc * TT + sub * 128 + 128],
                                        wv[:, kc * 512:(kc + 1) * 512],
                                        start=(kc == 0), stop=(kc == 7))
                    pev = p.inc(mm, "pe")
                    p.wait(A, "pe", pev)
                    vs = vst[:, :]
                    p.waitt(A, ev_free.get("vw", ("pe", 0)))
                    p.wait(A, "dv", ev_free.get("vg", 0))
                    ev = p.inc(A.activation(out=vs, in_=pv, func=AF.Copy), "ac")
                    ev_free[("v", sub % 2)] = ev
                    p.wait(SC, "ac", ev)
                    jc = t * 4 + sub
                    ev_free["vw"] = p.dma(SC, f"q_vw{sub % 2}", self.vd[jc], vs)
                    if t == NT - 1:
                        p.wait(V, "ac", ev)
                        p.waitt(V, ev_free.get("hv", ("pe", 0)))
                        dg = p.inc(V.tensor_scalar(out=gate[:, :], in0=vs, scalar1=self.wgate,
                                                   scalar2=None, op0=OP.mult), "dv")
                        ev_free["vg"] = dg
                        p.wait(G, "dv", dg)
                        ev_free["hv"] = p.gdma("g_hv", self.vhalo_in[sub], gate[:, :])
                p.wait(TEN, "dv", hdone)
                for qk in range(2):
                    nbank = self.bank(2 + qk)
                    ind = indqs if qk == 0 else indks
                    cs, ss = tbs[0], tbs[1]
                    p.wait(Y, "dv", tbl_free[0])
                    p.wait(Y, "po", tbl_free[1])
                    tl = p.dma(Y, "q_tb", cs, self.inp["csq" if qk == 0 else "csk"][:, c0:c0 + TT])
                    tl = p.dma(Y, "q_tb", ss, self.inp["ssq" if qk == 0 else "ssk"][:, c0:c0 + TT])
                    for oc in range(4):
                        pqk = self.bank(oc % 2)
                        p.waitt(TEN, wq_ld)
                        p.wait(TEN, "ac", ev_free.get(("qk", oc % 2), 0))
                        woc = 4 * qk + oc
                        for kc in range(8):
                            mm = TEN.matmul(pqk,
                                            wqk[:, (woc * 8 + kc) * 128:(woc * 8 + kc) * 128 + 128],
                                            hc[kc], start=(kc == 0), stop=(kc == 7))
                        peq = p.inc(mm, "pe")
                        p.wait(A, "pe", peq)
                        sq = p.inc(A.activation(out=sqn[:, :], in_=pqk, func=AF.Square), "ac")
                        p.wait(TEN, "ac", sq)
                        mm = TEN.matmul(nbank[0:8, :], self.ind8Ts[:, oc * 8:oc * 8 + 8],
                                        sqn[:, :], start=(oc == 0), stop=(oc == 3))
                        p.inc(mm, "pe")
                        qro = qraw[:, oc * 512:(oc + 1) * 512]
                        p.wait(A, "dv", ev_free.get(("qraw", oc), 0))
                        ev = p.inc(A.activation(out=qro, in_=pqk, func=AF.Copy), "ac")
                        ev_free[("qk", oc % 2)] = ev
                    p.wait(A, "pe")
                    p.wait(A, "ac")
                    a1 = p.inc(A.activation(out=nf[0:8, :], in_=nbank[0:8, :], func=AF.Ln,
                                            bias=1e-18, scale=1.0), "ac")
                    p.wait(A, "ac", a1)
                    a2 = p.inc(A.activation(out=nb[0:8, :], in_=nf[0:8, :], func=AF.Exp,
                                            scale=-0.5), "ac")
                    for oc in range(4):
                        woc = 4 * qk + oc
                        qro = qraw[:, oc * 512:(oc + 1) * 512]
                        pb = self.bank(4)
                        pr = self.bank(5)
                        p.wait(TEN, "ac", a2)
                        p.wait(TEN, "dv", pb_free)
                        mm = TEN.matmul(pb, ind[0:8, oc * 128:(oc + 1) * 128], nb[0:8, :],
                                        start=True, stop=True)
                        peb = p.inc(mm, "pe")
                        p.wait(V, "pe", peb)
                        p.wait(V, "ac", ev_free.get(("qk", oc % 2), 0))
                        p.wait(V, "po", qn_free)
                        d1 = p.inc(V.tensor_tensor(out=qn[:, :], in0=qro, in1=pb, op=OP.mult), "dv")
                        pb_free = d1
                        p.wait(TEN, "dv", d1)
                        mm = TEN.matmul(pr, self.PTf[:, :], qn[:, :], start=True, stop=True)
                        per = p.inc(mm, "pe")
                        p.wait(V, "pe", per)
                        p.waitt(V, tl)
                        d2 = p.inc(V.tensor_tensor(out=t1b[:, :], in0=pr, in1=ss, op=OP.mult), "dv")
                        p.wait(PO, "dv", d1)
                        p.waitt(PO, tl)
                        d3 = p.inc(PO.tensor_tensor(out=qro, in0=qn[:, :], in1=cs,
                                                    op=OP.mult), "po")
                        qn_free = d3
                        p.wait(V, "dv", d2)
                        p.wait(V, "po", d3)
                        qfo = qf[:, (oc % 2) * 512:(oc % 2) * 512 + 512]
                        p.waitt(V, ev_free.get(("qf", oc % 2), ("pe", 0)))
                        d4 = p.inc(V.tensor_tensor(out=qfo, in0=t1b[:, :], in1=qro,
                                                   op=OP.add), "dv")
                        ev_free[("qraw", oc)] = d4
                        p.wait(SC, "dv", d4)
                        if qk == 0:
                            ev_free[("qf", oc % 2)] = p.dma(SC, f"q_qw{oc % 2}",
                                                            self.qd[oc][:, c0:c0 + TT], qfo)
                        else:
                            ev_free[("qf", oc % 2)] = p.dma(
                                SC, f"q_kw{oc % 2}",
                                self.kd[t * 4:t * 4 + 4, oc].rearrange("j p c -> p j c"),
                                qfo.rearrange("p (j c) -> p j c", j=4))
                            if t == NT - 1:
                                p.wait(V, "dv", d4)
                                p.waitt(V, ev_free.get("hk", ("pe", 0)))
                                p.waitt(V, ev_free.get("hv", ("pe", 0)))
                                p.waitt(V, ev_free[("qf", oc % 2)])
                                dg = p.inc(V.tensor_scalar(out=gate[:, :], in0=qfo,
                                                           scalar1=self.wgate, scalar2=None,
                                                           op0=OP.mult), "dv")
                                p.wait(G, "dv", dg)
                                ev_free["hk"] = p.gdma(
                                    "g_hk", self.khalo_in[:, oc].rearrange("j p c -> p j c"),
                                    gate[:, :].rearrange("p (j c) -> p j c", j=4))
                    tbl_free[0] = p.cnt["dv"]
                    tbl_free[1] = p.cnt["po"]
                h_free[bi] = p.cnt["pe"]
            p.wait(G, "cc")
            for tok in (ev_free.get("hv"), ev_free.get("hk")):
                if tok is not None:
                    p.waitt(G, tok)
            p.inc(G.collective_compute("AllReduce", OP.add, replica_groups=RG,
                                       ins=[self.khalo_in], outs=[self.khalo]), "cc")
            cc = p.inc(G.collective_compute("AllReduce", OP.add, replica_groups=RG,
                                            ins=[self.vhalo_in], outs=[self.vhalo]), "cc")
            self.fence()
        return cc

    def emit_ffn_attn(self, l):
        nc, p = self.nc, self.p
        TEN, V, A, G, Y, PO = nc.tensor, nc.vector, nc.scalar, nc.gpsimd, nc.sync, nc.gpsimd
        SC = nc.scalar
        with contextlib.ExitStack() as ph:
            ent = ph.enter_context
            masksb = ent(nc.sbuf_tensor(self.uname("masksb"), [128, 25 * 128], BF16))
            self.masksb = masksb
            wos = ent(nc.sbuf_tensor(self.uname("wos"), [128, 4096], BF16))
            qst = ent(nc.sbuf_tensor(self.uname("qst"), [128, 2 * 2048], BF16))
            kst = ent(nc.sbuf_tensor(self.uname("kst"), [128, 8 * 512], BF16))
            vpl = ent(nc.sbuf_tensor(self.uname("vpl"), [128, 8 * 520], BF16))
            vtmp = ent(nc.sbuf_tensor(self.uname("vtmp"), [128, 512], BF16))
            asb = ent(nc.sbuf_tensor(self.uname("asb"), [128, 2 * 640], BF16))
            obf = ent(nc.sbuf_tensor(self.uname("obf"), [128, 2048], F32))
            ofin = ent(nc.sbuf_tensor(self.uname("ofin"), [128, 2048], BF16))
            zr8 = ent(nc.sbuf_tensor(self.uname("zr8"), [8, 512], F32))
            rz8 = ent(nc.sbuf_tensor(self.uname("rz8"), [8, 512], F32))
            self._zslot_free = {}
            self._zr_free = 0
            self._rz_free = 0
            mld = p.dma(Y, "q_mask", masksb[:, :].rearrange("p (a b) -> p a b", a=25),
                        self.inp["masks"].rearrange("x y p q -> p (x y) q"))
            for oc in range(4):
                wld = p.gdma("g_wo", wos[:, oc * 1024:(oc + 1) * 1024]
                             .rearrange("p (xc c) -> p xc c", xc=8),
                             self.inp["wo"][l][oc * 128:(oc + 1) * 128, :]
                             .rearrange("p (xc c) -> p xc c", xc=8))
            stA = self.ps[0][:, 0:640]
            stB = self.ps[1][:, 0:640]
            oA = self.ps[2][0:65, 0:512]
            oB = self.ps[2][0:65, 512:1024]
            zb = self.ps[0][:, 0:512]
            slot_free = {}
            q_free = {}
            st_free = {0: 0, 1: 0}
            asb_free = {0: 0, 1: 0}
            o_free = 0
            zb_free = 0
            psx_free = {0: 0, 1: 0}
            obf_free = 0
            ofin_free = 0
            loaded = {}
            self._kld = {}
            self._kgd = {}
            self._vtmp_free = 0
            worder = [7, 0, 1, 2, 3, 4, 5, 6]
            for wi, cqg in enumerate(worder):
                qb = wi % 2
                qsl = qst[:, qb * 2048:(qb + 1) * 2048]
                p.wait(Y, "pe", q_free.get(qb, 0))
                qld = p.dma(Y, f"q_qs{qb}", qsl.rearrange("p (c t) -> p c t", c=4),
                            self.qd[:, :, cqg * 512:(cqg + 1) * 512].rearrange("c p t -> p c t"))
                for j in range(4 * cqg - 4, 4 * cqg + 4):
                    sl = j % 8
                    if loaded.get(sl) == j:
                        continue
                    loaded[sl] = j
                    p.wait(Y, "pe", slot_free.get(sl, 0))
                    p.wait(G, "pe", slot_free.get(sl, 0))
                    ksrc = self.kd[j] if j >= 0 else self.khalo[j + 4]
                    vsrc = self.vd[j] if j >= 0 else self.vhalo[j + 4]
                    self._kld[sl] = p.dma(Y, f"q_ks{sl}", kst[:, sl * 512:(sl + 1) * 512]
                                          .rearrange("p (c x) -> p c x", c=4),
                                          ksrc.rearrange("c p x -> p c x"))
                    if j < 0:
                        p.waitt(V, self._kld[sl])
                        p.wait(V, "pe", slot_free.get(sl, 0))
                        kgd = p.inc(V.tensor_scalar(out=kst[:, sl * 512:(sl + 1) * 512],
                                                    in0=kst[:, sl * 512:(sl + 1) * 512],
                                                    scalar1=self.rgate, scalar2=None,
                                                    op0=OP.mult), "dv")
                        self._kgd[sl] = kgd
                    p.wait(G, "dv", self._vtmp_free)
                    vl = p.dma(G, f"g_vs{sl}", vtmp[:, :], vsrc)
                    p.waitt(V, vl)
                    p.wait(V, "pe", slot_free.get(sl, 0))
                    vslot = vpl[:, sl * 520:(sl + 1) * 520]
                    m1 = p.inc(V.memset(vslot, 1.0), "dv")
                    p.wait(V, "dv", m1)
                    self._vtmp_free = p.inc(
                        V.tensor_copy(out=vslot.rearrange("p (h c) -> p h c", h=8)[:, :, 0:64],
                                      in_=vtmp[:, :].rearrange("p (h c) -> p h c", h=8)), "dv")
                vload_dv = p.cnt["dv"]
                if "sdbg" in self.taps and wi == 0:
                    dq = self.nc.dram_tensor("dbg_qsl", [128, 2048], BF16,
                                             kind="ExternalOutput").ap()
                    dk = self.nc.dram_tensor("dbg_kst", [128, 4096], BF16,
                                             kind="ExternalOutput").ap()
                    p.waitt(Y, qld)
                    for s2 in range(8):
                        p.waitt(Y, self._kld[s2])
                    p.wait(Y, "dv", vload_dv)
                    p.dma(Y, "q_tap", dq, qsl)
                    p.dma(Y, "q_tap", dk, kst[:, :])
                for c in range(4):
                    hA, hB = 2 * c, 2 * c + 1
                    for cq in range(4):
                        cqglob = cqg * 4 + cq
                        p.waitt(TEN, qld)
                        for cr in range(5):
                            p.waitt(TEN, self._kld[(cqglob - 4 + cr) % 8])
                            p.wait(TEN, "dv", self._kgd.get((cqglob - 4 + cr) % 8, 0))
                        p.wait(TEN, "ac", st_free[0])
                        p.wait(TEN, "ac", st_free[1])
                        p.wait(TEN, "dv", zb_free)
                        for cr in range(5):
                            j = cqglob - 4 + cr
                            sl = j % 8
                            rhA = qsl[0:64, c * 512 + cq * 128: c * 512 + cq * 128 + 128]
                            rhB = qsl[64:128, c * 512 + cq * 128: c * 512 + cq * 128 + 128]
                            lhA = kst[0:64, sl * 512 + c * 128: sl * 512 + c * 128 + 128]
                            lhB = kst[64:128, sl * 512 + c * 128: sl * 512 + c * 128 + 128]
                            TEN.matmul(stA[:, cr * 128:(cr + 1) * 128], lhA, rhA,
                                       start=True, stop=True, tile_position=(0, 0))
                            mm = TEN.matmul(stB[:, cr * 128:(cr + 1) * 128], lhB, rhB,
                                            start=True, stop=True, tile_position=(64, 0))
                        peS = p.inc(mm, "pe")
                        slot_free.update({(cqglob - 4 + cr) % 8: peS for cr in range(5)})
                        q_free[qb] = peS
                        if "sdbg" in self.taps and wi == 0 and c == 0 and cq == 0:
                            dst_d = nc.dram_tensor("dbg_st", [128, 1280], F32,
                                                   kind="ExternalOutput").ap()
                            dng = nc.dram_tensor("dbg_nc", [128, 1], F32,
                                                 kind="ExternalOutput").ap()
                            p.wait(A, "pe", peS)
                            p.inc(A.activation(out=ofin[:, 0:640], in_=stA, func=AF.Copy), "ac")
                            x1 = p.inc(A.activation(out=ofin[:, 640:1280], in_=stB,
                                                    func=AF.Copy), "ac")
                            p.wait(G, "ac", x1)
                            tk1 = p.gdma("g_tap", dst_d[:, 0:640], ofin[:, 0:640])
                            tk1 = p.gdma("g_tap", dst_d[:, 640:1280], ofin[:, 640:1280])
                            tk2 = p.gdma("g_tap", dng, self.negC)
                            p.waitt(V, tk1)
                            p.waitt(A, tk1)
                        p.wait(A, "pe", peS)
                        p.wait(A, "pe", asb_free[0])
                        p.wait(A, "pe", asb_free[1])
                        eA = p.inc(A.activation(out=asb[:, 0:640], in_=stA, func=AF.Exp,
                                                bias=self.negC, scale=1.0), "ac")
                        eB = p.inc(A.activation(out=asb[:, 640:1280], in_=stB, func=AF.Exp,
                                                bias=self.negC, scale=1.0), "ac")
                        st_free[0], st_free[1] = eA, eB
                        cls = cqglob if cqglob < 4 else 4
                        crs = range(5) if cqglob < 4 else (0, 4)
                        p.waitt(V, mld)
                        p.wait(V, "ac", eB)
                        last = None
                        for cr in crs:
                            mk = self.mask(cls, cr)
                            last = V.tensor_tensor(out=asb[:, cr * 128:(cr + 1) * 128],
                                                   in0=asb[:, cr * 128:(cr + 1) * 128],
                                                   in1=mk, op=OP.mult)
                            last = V.tensor_tensor(out=asb[:, 640 + cr * 128:640 + (cr + 1) * 128],
                                                   in0=asb[:, 640 + cr * 128:640 + (cr + 1) * 128],
                                                   in1=mk, op=OP.mult)
                        dvM = p.inc(last, "dv")
                        p.wait(TEN, "dv", dvM)
                        p.wait(TEN, "dv", vload_dv)
                        p.wait(TEN, "ac", o_free)
                        for cr in range(5):
                            j = cqglob - 4 + cr
                            sl = j % 8
                            TEN.matmul(oA[:, cq * 128:(cq + 1) * 128],
                                       vpl[:, sl * 520 + hA * 65: sl * 520 + hA * 65 + 65],
                                       asb[:, cr * 128:(cr + 1) * 128],
                                       start=(cr == 0), stop=(cr == 4))
                            mm = TEN.matmul(oB[:, cq * 128:(cq + 1) * 128],
                                            vpl[:, sl * 520 + hB * 65: sl * 520 + hB * 65 + 65],
                                            asb[:, 640 + cr * 128:640 + (cr + 1) * 128],
                                            start=(cr == 0), stop=(cr == 4))
                        peav = p.inc(mm, "pe")
                        asb_free[0] = asb_free[1] = peav
                        slot_free.update({(cqglob - 4 + cr) % 8: peav for cr in range(5)})
                    p.wait(A, "pe", peav)
                    p.wait(A, "dv", obf_free)
                    p.wait(A, "pe", ofin_free)
                    for hh, ob in ((hA, oA), (hB, oB)):
                        dst = obf[:, c * 512:(c + 1) * 512]
                        dsv = dst[0:64, :] if hh == hA else dst[64:128, :]
                        p.inc(A.activation(out=dsv, in_=ob[0:64, :], func=AF.Copy), "ac")
                        zs = self.row[:, (hh % 2) * 512:(hh % 2) * 512 + 512]
                        p.waitt(A, self._zslot_free.get(hh % 2, ("pe", 0)))
                        az = p.inc(A.activation(out=zs, in_=ob[64:65, :], func=AF.Copy), "ac")
                        p.wait(Y, "ac", az)
                        self._zslot_free[hh % 2] = p.dma(Y, f"q_zw{hh % 2}", self.zrowd[hh], zs)
                    o_free = p.cnt["ac"]
                p.wait(Y, "q_zw0")
                p.wait(Y, "q_zw1")
                p.wait(Y, "ac", self._zr_free)
                p.wait(Y, "pe", self._rz_free)
                zl = p.dma(Y, "q_zr", zr8[:, :], self.zrowd)
                p.waitt(A, zl)
                a1 = p.inc(A.activation(out=zr8[:, :], in_=zr8[:, :], func=AF.Ln), "ac")
                p.wait(A, "ac", a1)
                p.wait(A, "pe", self._rz_free)
                a2 = p.inc(A.activation(out=rz8[:, :], in_=zr8[:, :], func=AF.Exp,
                                        scale=-1.0), "ac")
                self._zr_free = a2
                dlast = None
                for c in range(4):
                    p.wait(TEN, "ac", a2)
                    p.wait(TEN, "dv", zb_free)
                    mm = TEN.matmul(zb, self.ind8s[0:8, c * 128:(c + 1) * 128], rz8[:, :],
                                    start=True, stop=True)
                    pez = p.inc(mm, "pe")
                    self._rz_free = pez
                    p.wait(V, "pe", pez)
                    p.wait(V, "ac", o_free)
                    p.wait(V, "pe", ofin_free)
                    dlast = p.inc(V.tensor_tensor(out=ofin[:, c * 512:(c + 1) * 512],
                                                  in0=obf[:, c * 512:(c + 1) * 512],
                                                  in1=zb, op=OP.mult), "dv")
                    zb_free = dlast
                obf_free = dlast
                if "odbg" in self.taps:
                    if not hasattr(self, "_odbg"):
                        self._odbg = nc.dram_tensor("dbg_o", [8, 128, 2048], BF16,
                                                    kind="ExternalOutput").ap()
                        self._obdbg = nc.dram_tensor("dbg_ob", [8, 128, 2048], BF16,
                                                     kind="ExternalOutput").ap()
                        self._zdbg = nc.dram_tensor("dbg_z", [8, 8, 512], F32,
                                                    kind="ExternalOutput").ap()
                    p.wait(Y, "dv", zb_free)
                    p.wait(Y, "ac", o_free)
                    otap = p.dma(Y, "q_tapA", self._odbg[cqg], ofin[:, :])
                    p.waitt(V, otap)
                    otap2 = p.dma(Y, "q_tapB", self._obdbg[cqg], obf[:, :])
                    p.waitt(V, otap2)
                    p.waitt(A, p.dma(Y, "q_tapC", self._zdbg[cqg], zr8[:, :]))
                p.waitt(TEN, wld)
                c0 = cqg * TT
                for xc in range(NCH):
                    px = self.ps[3][:, (xc % 2) * 512:(xc % 2) * 512 + 512]
                    p.wait(TEN, "dv", max(zb_free, psx_free[xc % 2]))
                    for oc in range(4):
                        mm = TEN.matmul(px, wos[:, (oc * 8 + xc) * 128:(oc * 8 + xc) * 128 + 128],
                                        ofin[:, oc * 512:(oc + 1) * 512],
                                        start=(oc == 0), stop=(oc == 3))
                    pex = p.inc(mm, "pe")
                    ofin_free = pex
                    p.wait(V, "pe", pex)
                    yt = self.ytile(xc, c0)
                    dyu = p.inc(V.scalar_tensor_tensor(out=yt, in0=px, scalar=0.5, in1=yt,
                                                       op0=OP.mult, op1=OP.add), "dv")
                    psx_free[xc % 2] = dyu
                if cqg == 7:
                    self.emit_ycol_exchange()
            self.fence()

    def emit_ffn_mlp(self, l):
        nc, p = self.nc, self.p
        TEN, V, A, G, Y = nc.tensor, nc.vector, nc.scalar, nc.gpsimd, nc.sync
        NJ = 22
        segs = [(0, 6), (6, 12), (12, 18), (18, 22)]
        mof = lambda jj: 128 if jj < NJ - 1 else 42
        with contextlib.ExitStack() as ph:
            ent = ph.enter_context
            gcol = ent(nc.sbuf_tensor(self.uname("m_gc"), [128, 64], F32))
            gb2 = ent(nc.sbuf_tensor(self.uname("m_gb2"), [2, 1024], BF16))
            h1t = [ent(nc.sbuf_tensor(self.uname(f"m_h1{i}"), [128, NCH * TT], BF16)) for i in range(2)]
            pha = contextlib.ExitStack()
            enta = pha.enter_context
            hb = enta(nc.sbuf_tensor(self.uname("m_hb"), [128, NCH * TT], BF16))
            sqb = enta(nc.sbuf_tensor(self.uname("m_sq"), [128, 2 * TT], BF16))
            rr2 = enta(nc.sbuf_tensor(self.uname("m_rr2"), [2, 512], BF16))
            t0 = enta(nc.sbuf_tensor(self.uname("m_t0"), [128, TT], F32))
            nbr = self.row[:, 0:1024]
            pl = p.dma(Y, "q_gb", nbr, self.inp["ln_b"][l:l + 1, :])
            p.waitt(V, pl)
            p.wait(V, "dv")
            nb1 = p.inc(V.tensor_scalar(out=nbr, in0=nbr, scalar1=-1.0, scalar2=None,
                                        op0=OP.mult), "dv")
            p.wait(G, "dv", nb1)
            gw = p.gdma("g_gb", self.gbd[1:2, :], nbr)
            with nc.allow_non_contiguous_dma(reason="one-time per-layer column loads"):
                gl = p.dma(Y, "q_gc", gcol[:, 0:8],
                           self.inp["ln_g"][l].rearrange("(c p) -> p c", p=128))
                p.waitt(Y, gw)
                g2a = p.gdma("g_gb2", gb2[0:1, :], self.inp["ln_g"][l:l + 1, :])
                g2b = p.dma(Y, "q_gb2b", gb2[1:2, :], self.gbd[1])
                bl1 = p.dma(Y, "q_gc", gcol[:, 8:8 + 21],
                            self.inp["b1"][l][0:2688].rearrange("(c p) -> p c", p=128))
                bl2 = p.dma(Y, "q_gc", gcol[0:42, 29:30], self.inp["b1"][l][2688:2730]
                            .rearrange("(c p) -> p c", p=42))
                bl3 = p.dma(Y, "q_gc", gcol[:, 30:30 + 21],
                            self.inp["b1"][l][2730:2730 + 2688].rearrange("(c p) -> p c", p=128))
                bl4 = p.dma(Y, "q_gc", gcol[0:42, 51:52], self.inp["b1"][l][2730 + 2688:5460]
                            .rearrange("(c p) -> p c", p=42))
            gc_all = ("q_gc", p.cnt["q_gc"])
            p.inc(V.memset(rr2[:, :], 1.0), "dv")
            b1a = lambda jj, m: gcol[0:m, 8 + jj:9 + jj]
            b1g = lambda jj, m: gcol[0:m, 30 + jj:31 + jj]
            p.waitt(V, gw)
            s1 = self.bank(0)[0:1, :]
            s2 = self.bank(1)[0:1, :]
            rb = self.bank(2)
            tbps = [self.bank(3), self.bank(4)]
            dv_prev = 0
            pe_prev = 0
            peb2 = 0
            h1w = {}
            sq_free = {0: 0, 1: 0}
            dvh_par = {0: 0, 1: 0}
            for t in [1, 2, 3, 4, 5, 6, 7, 0]:
                c0 = t * TT
                hc = self.chunks(hb)
                p.wait(V, "pe", pe_prev)
                p.wait(V, "dv")
                ctok = []
                for c in range(NCH):
                    yt = self.ytile(c, c0) if c < 4 else self.ytile(c, c0, shift=1)
                    cp = V.tensor_scalar(out=hc[c], in0=yt, scalar1=1.0, scalar2=None,
                                         op0=OP.mult)
                    ctok.append(p.inc(cp, "dv"))
                p.wait(TEN, "dv", dv_prev)
                p.wait(TEN, "pe", pe_prev)
                for c in range(NCH):
                    sl = sqb[:, (c % 2) * TT:(c % 2) * TT + TT]
                    p.wait(A, "pe", sq_free[c % 2])
                    p.wait(A, "dv", ctok[c])
                    aq = p.inc(A.activation(out=sl, in_=hc[c], func=AF.Square), "ac")
                    p.wait(TEN, "dv", ctok[c])
                    TEN.matmul(s1, self.ones128b, hc[c], start=(c == 0), stop=(c == NCH - 1))
                    p.wait(TEN, "ac", aq)
                    mm = TEN.matmul(s2, self.ones128b, sl, start=(c == 0), stop=(c == NCH - 1))
                    sq_free[c % 2] = p.inc(mm, "pe")
                pe_stats = p.cnt["pe"]
                r = self.row
                mu, t1r, t2r = (r[:, i * TT:(i + 1) * TT] for i in range(3))
                p.wait(V, "pe", pe_stats)
                d1 = p.inc(V.tensor_scalar(out=mu, in0=s1, scalar1=1.0 / D, scalar2=None,
                                           op0=OP.mult), "dv")
                p.wait(V, "dv", d1)
                d2 = p.inc(V.tensor_tensor(out=t1r, in0=mu, in1=mu, op=OP.mult), "dv")
                p.wait(V, "dv", d2)
                d3 = p.inc(V.scalar_tensor_tensor(out=t1r, in0=s2, scalar=1.0 / D, in1=t1r,
                                                  op0=OP.mult, op1=OP.subtract), "dv")
                p.wait(A, "dv", d3)
                a1 = p.inc(A.activation(out=t2r, in_=t1r, func=AF.Ln, bias=1e-5, scale=1.0), "ac")
                p.wait(A, "ac", a1)
                a2 = p.inc(A.activation(out=t2r, in_=t2r, func=AF.Exp, scale=-0.5), "ac")
                rp = t2r
                p.wait(V, "ac", a2)
                d4 = p.inc(V.tensor_tensor(out=t1r, in0=mu, in1=rp, op=OP.mult), "dv")
                sp = t1r
                p.wait(V, "dv", d4)
                p.wait(V, "pe", peb2)
                d5 = p.inc(V.tensor_copy(out=rr2[0:1, :], in_=sp), "dv")
                p.wait(TEN, "ac", a2)
                p.wait(TEN, "dv", d5)
                mmb = TEN.matmul(rb, self.onesrow[:, :], rp, start=True, stop=True)
                pe_b = p.inc(mmb, "pe")
                pe_prev = pe_b
                bi = t % 2
                hh = h1t[bi]
                h1c = [hh[:, c * TT:(c + 1) * TT] for c in range(NCH)]
                p.waitt(V, h1w.get(bi, ("pe", 0)))
                p.waitt(V, gc_all)
                for c in range(NCH):
                    tb = tbps[c % 2]
                    p.waitt(TEN, g2a)
                    p.waitt(TEN, g2b)
                    p.wait(TEN, "dv", dvh_par[c % 2])
                    mm2 = TEN.matmul(tb[:, 0:TT], gb2[:, c * 128:(c + 1) * 128], rr2[:, :],
                                     start=True, stop=True)
                    peb2 = p.inc(mm2, "pe")
                    p.wait(V, "pe", peb2)
                    dt0 = p.inc(V.tensor_tensor(out=t0[:, :], in0=hc[c], in1=rb, op=OP.mult), "dv")
                    p.wait(V, "dv", dt0)
                    last = V.scalar_tensor_tensor(out=h1c[c], in0=t0[:, :],
                                                  scalar=gcol[:, c:c + 1], in1=tb[:, 0:TT],
                                                  op0=OP.mult, op1=OP.subtract)
                    dvh_par[c % 2] = p.inc(last, "dv")
                dv_prev = p.cnt["dv"]
                p.wait(Y, "dv", dv_prev)
                h1w[bi] = p.dma(Y, f"q_h1w{bi}", self.fm(self.h1d, c0),
                                hh[:, :].rearrange("p (c t) -> p c t", c=NCH))
            self.fence()
            pha.close()
            phb = contextlib.ExitStack()
            entb = phb.enter_context
            b2r = entb(nc.sbuf_tensor(self.uname("m_b2r"), [1, 1024], BF16))
            gsb = entb(nc.sbuf_tensor(self.uname("m_gsb"), [128, 2 * TT], F32))
            h2s = entb(nc.sbuf_tensor(self.uname("m_h2"), [128, 6 * TT], BF16))
            w1s = entb(nc.sbuf_tensor(self.uname("m_w1"), [128, 12 * 1024], BF16))
            w2s = entb(nc.sbuf_tensor(self.uname("m_w2"), [128, 6 * 1024], BF16))
            b2l = p.gdma("g_b2", b2r[:, :], self.inp["b2"][l:l + 1, :])
            h1_free = {0: 0, 1: 0}
            pa_free = {0: 0, 1: 0, 2: 0}
            pg_free = {0: 0, 1: 0, 2: 0}
            px_free = {0: 0, 1: 0}
            h2_free = 0
            gsb_free = {0: 0, 1: 0}
            w_free = 0
            w1sl = lambda sl, kc, m: w1s[:, sl * 2048 + kc * m: sl * 2048 + (kc + 1) * m]
            w1gl = lambda sl, kc, m: w1s[:, sl * 2048 + 1024 + kc * m: sl * 2048 + 1024 + (kc + 1) * m]
            for si, (j0, j1) in enumerate(segs):
                p.wait(G, "pe", w_free)
                jtok = {}
                for jj in range(j0, j1):
                    m = mof(jj)
                    sl = jj - j0
                    p.gdma(f"g_wj{sl}", w1s[:, sl * 2048:sl * 2048 + 8 * m]
                           .rearrange("p (kc c) -> p kc c", kc=8),
                           self.inp["w1"][l][:, jj * 128:jj * 128 + m]
                           .rearrange("(kc p) c -> p kc c", p=128))
                    p.gdma(f"g_wj{sl}", w1s[:, sl * 2048 + 1024:sl * 2048 + 1024 + 8 * m]
                           .rearrange("p (kc c) -> p kc c", kc=8),
                           self.inp["w1"][l][:, HID + jj * 128:HID + jj * 128 + m]
                           .rearrange("(kc p) c -> p kc c", p=128))
                    tok = p.gdma(f"g_wj{sl}", w2s[0:m, sl * 1024:(sl + 1) * 1024]
                                 .rearrange("p (xc c) -> p xc c", xc=8),
                                 self.inp["w2"][l][jj * 128:jj * 128 + m, :]
                                 .rearrange("p (xc c) -> p xc c", xc=8))
                    jtok[sl] = tok
                for t in [7, 0, 1, 2, 3, 4, 5, 6]:
                    c0 = t * TT
                    bi = t % 2
                    hh = h1t[bi]
                    h1c = [hh[:, c * TT:(c + 1) * TT] for c in range(NCH)]
                    p.wait(Y, "pe", h1_free[bi])
                    p.waitt(Y, h1w[bi])
                    hl = p.dma(Y, f"q_h1r{bi}", hh[:, :].rearrange("p (c t) -> p c t", c=NCH),
                               self.fm(self.h1d, c0))
                    p.waitt(TEN, hl)
                    dh2_sl = {}
                    for jj in range(j0, j1):
                        m = mof(jj)
                        sl = jj - j0
                        par = sl % 3
                        gpar = sl % 2
                        pa = self.bank((0, 1, 6)[par])
                        pg = self.bank((2, 3, 7)[par])
                        gsl = gsb[0:m, gpar * TT:gpar * TT + TT]
                        p.waitt(TEN, jtok[sl])
                        p.wait(TEN, "dv", pa_free[par])
                        p.wait(TEN, "ac", pg_free[par])
                        p.wait(TEN, "pe", h2_free if t != 7 or si > 0 else 0)
                        for kc in range(8):
                            TEN.matmul(pa[0:m, :], w1sl(sl, kc, m), h1c[kc],
                                       start=(kc == 0), stop=(kc == 7))
                        for kc in range(8):
                            mm = TEN.matmul(pg[0:m, :], w1gl(sl, kc, m), h1c[kc],
                                            start=(kc == 0), stop=(kc == 7))
                        peg = p.inc(mm, "pe")
                        p.wait(A, "pe", peg)
                        p.waitt(A, gc_all)
                        p.waitt(A, gc_all)
                        p.wait(A, "dv", gsb_free[gpar])
                        ga = p.inc(A.activation(out=gsl, in_=pg[0:m, :], func=AF.Gelu,
                                                bias=b1g(jj, m), scale=1.0), "ac")
                        pg_free[par] = ga
                        p.wait(V, "ac", ga)
                        p.waitt(V, gc_all)
                        p.wait(V, "pe", h2_free)
                        dh2 = p.inc(V.scalar_tensor_tensor(out=h2s[0:m, sl * TT:(sl + 1) * TT],
                                                           in0=pa[0:m, :], scalar=b1a(jj, m),
                                                           in1=gsl, op0=OP.add,
                                                           op1=OP.mult), "dv")
                        gsb_free[gpar] = dh2
                        pa_free[par] = dh2
                        dh2_sl[sl] = dh2
                    for xc in range(NCH):
                        par = xc % 2
                        px = self.bank(4 + par)
                        p.wait(TEN, "dv", px_free[par])
                        for jj in range(j0, j1):
                            m = mof(jj)
                            sl = jj - j0
                            p.waitt(TEN, jtok[sl])
                            p.wait(TEN, "dv", dh2_sl[sl])
                            stop = (si != 0 and jj == j1 - 1)
                            mm = TEN.matmul(px, w2s[0:m, sl * 1024 + xc * 128:sl * 1024 + (xc + 1) * 128],
                                            h2s[0:m, sl * TT:(sl + 1) * TT],
                                            start=(jj == j0), stop=stop)
                        if si == 0:
                            p.waitt(TEN, b2l)
                            mm = TEN.matmul(px, b2r[:, xc * 128:(xc + 1) * 128], self.onesb[:, :],
                                            start=False, stop=True)
                        pex = p.inc(mm, "pe")
                        h2_free = pex
                        p.wait(V, "pe", pex)
                        yt = self.ytile(xc, c0)
                        dyu = p.inc(V.scalar_tensor_tensor(out=yt, in0=px, scalar=0.5, in1=yt,
                                                           op0=OP.mult, op1=OP.add), "dv")
                        px_free[par] = dyu
                    h1_free[bi] = p.cnt["pe"]
                w_free = p.cnt["pe"]
            self.fence()
            phb.close()

    def bcast_col(self, scal, dst, stage2):
        nc, p = self.nc, self.p
        TEN, V, A = nc.tensor, nc.vector, nc.scalar
        p.wait(V, "dv")
        m0 = p.inc(V.memset(stage2, 0.0), "dv")
        p.wait(V, "dv", m0)
        m1 = p.inc(V.tensor_copy(out=stage2[0:1, 0:1], in_=scal), "dv")
        p.wait(TEN, "dv", m1)
        p.wait(TEN, "ac")
        pb = self.ps[3][:, 768:770]
        mm = TEN.matmul(pb, self.onesb[0:1, 0:128], stage2, start=True, stop=True)
        pe1 = p.inc(mm, "pe")
        p.wait(A, "pe", pe1)
        return p.inc(A.activation(out=dst, in_=pb[:, 0:1], func=AF.Copy), "ac")

    def emit_global_attn(self):
        nc, p = self.nc, self.p
        TEN, V, A, G, Y = nc.tensor, nc.vector, nc.scalar, nc.gpsimd, nc.sync
        SC = nc.scalar
        p.mksem("cck")
        p.mksem("ccv")
        yb = self.y[:, :].bitcast(BF16)
        q_all = lambda dc: yb[:, dc * T:(dc + 1) * T]
        vres = lambda jc: yb[:, jc * 1024:(jc + 1) * 1024]
        qm2 = self.smalls[0:1, 32:33]
        tmax = self.smalls[0:1, 33:34]
        with contextlib.ExitStack() as ph:
            ent = ph.enter_context
            wT = ent(nc.sbuf_tensor(self.uname("ga_wT"), [128, 2 * 64 * 128], BF16))
            wst = ent(nc.sbuf_tensor(self.uname("ga_wst"), [128, 2048], BF16))
            h2t = [ent(nc.sbuf_tensor(self.uname("ga_h2"), [128, NCH * TT], BF16))
                   for i in range(2)]
            qkt = ent(nc.sbuf_tensor(self.uname("ga_qk"), [128, 1024], BF16))
            sqn = ent(nc.sbuf_tensor(self.uname("ga_sqn"), [128, 1024], BF16))
            bqk = ent(nc.sbuf_tensor(self.uname("ga_bqk"), [128, 16], F32))
            pt = self.ps[3][:, 512:768].bitcast(BF16)
            tr_ev = 0
            for part in range(2):
                for fc in range(8):
                    wl = p.gdma(f"g_gw{fc % 2}", wst[:, (fc % 2) * 1024:(fc % 2) * 1024 + 1024],
                                self.inp["attn_W"][part * 1024 + fc * 128:
                                                   part * 1024 + fc * 128 + 128, :])
                    p.waitt(TEN, wl)
                    for dc in range(8):
                        p.wait(TEN, "ac", tr_ev)
                        mm = TEN.transpose(pt[:, 0:128],
                                           wst[:, (fc % 2) * 1024 + dc * 128:
                                               (fc % 2) * 1024 + dc * 128 + 128], self.ident)
                        pet = p.inc(mm, "pe")
                        p.wait(A, "pe", pet)
                        slot = part * 64 + fc * 8 + dc
                        tr_ev = p.inc(A.activation(out=wT[:, slot * 128:(slot + 1) * 128],
                                                   in_=pt[:, 0:128], func=AF.Copy), "ac")
                    p.wait(G, "pe", p.cnt["pe"])
            with nc.allow_non_contiguous_dma(reason="one-time bias col loads"):
                bq = p.dma(Y, "q_gab", bqk[:, 0:8],
                           self.inp["attn_b"][0, 0:1024].rearrange("(c p) -> p c", p=128))
                bq = p.dma(Y, "q_gab", bqk[:, 8:16],
                           self.inp["attn_b"][0, 1024:2048].rearrange("(c p) -> p c", p=128))
            s1 = self.bank(4)[0:1, :]
            nslot = lambda i: self.row[:, (i % 2) * 512:(i % 2) * 512 + 512]
            ev_free = {}
            h_free = {0: 0, 1: 0}
            n_free = {}
            p.wait(TEN, "ac", tr_ev)
            for part in range(2):
                for t in range(NT):
                    bi = t % 2
                    c0 = t * TT
                    hb = h2t[bi]
                    hc = self.chunks(hb)
                    p.wait(Y, "pe", h_free[bi])
                    hl = p.dma(Y, f"q_gh{bi}", hb[:, :].rearrange("p (c t) -> p c t", c=NCH),
                               self.fm(self.h2d, c0))
                    p.waitt(TEN, hl)
                    for oc in range(8):
                        pq = self.bank(oc % 2)
                        p.wait(TEN, "ac", ev_free.get(("sq", oc % 2), 0))
                        for dc in range(8):
                            slot = part * 64 + oc * 8 + dc
                            mm = TEN.matmul(pq, wT[:, slot * 128:(slot + 1) * 128], hc[dc],
                                            start=(dc == 0), stop=(dc == 7))
                        peq = p.inc(mm, "pe")
                        h_free[bi] = peq
                        p.wait(A, "pe", peq)
                        p.waitt(A, bq)
                        qk = qkt[:, (oc % 2) * 512:(oc % 2) * 512 + 512]
                        p.waitt(A, ev_free.get(("qkw", oc % 2), ("pe", 0)))
                        p.wait(A, "pe", ev_free.get(("sqm", oc % 2), 0))
                        ev = p.inc(A.activation(out=qk, in_=pq, func=AF.Identity,
                                                bias=bqk[:, part * 8 + oc:part * 8 + oc + 1]), "ac")
                        p.wait(A, "ac", ev)
                        sq = sqn[:, (oc % 2) * 512:(oc % 2) * 512 + 512]
                        ev2 = p.inc(A.activation(out=sq, in_=qk, func=AF.Square), "ac")
                        ev_free[("sq", oc % 2)] = ev2
                        p.wait(TEN, "ac", ev2)
                        p.wait(TEN, "dv", n_free.get("srow", 0))
                        mm = TEN.matmul(s1, self.ones128b, sq, start=(oc == 0), stop=(oc == 7))
                        ev_free[("sqm", oc % 2)] = p.inc(mm, "pe")
                        p.wait(SC, "ac", ev)
                        p.wait(V, "ac", ev)
                        if part == 0:
                            d = p.inc(V.tensor_copy(out=q_all(oc)[:, c0:c0 + TT], in_=qk), "dv")
                            ev_free[("qkw", oc % 2)] = ("dv", d)
                        else:
                            ev_free[("qkw", oc % 2)] = p.dma(
                                SC, f"q_gkw{oc % 2}",
                                self.kgin[t * 4:t * 4 + 4, oc].rearrange("j p x -> p j x"),
                                qk.rearrange("p (j x) -> p j x", j=4))
                    pes = ev_free[("sqm", 7 % 2)]
                    p.wait(V, "pe", pes)
                    p.waitt(V, n_free.get(("nw", t % 2), ("pe", 0)))
                    d1 = p.inc(V.tensor_copy(out=nslot(t), in_=s1), "dv")
                    n_free["srow"] = d1
                    if part == 0:
                        p.wait(V, "dv", d1)
                        d2 = p.inc(V.reduce_max(out=tmax, in_=nslot(t),
                                                axis=mybir.AxisListType.X), "dv")
                        p.wait(V, "dv", d2)
                        if t == 0:
                            d3 = V.tensor_copy(out=qm2, in_=tmax)
                        else:
                            d3 = V.tensor_tensor(out=qm2, in0=qm2, in1=tmax, op=OP.max)
                        n_free["srow"] = p.inc(d3, "dv")
                    else:
                        p.wait(Y, "dv", d1)
                        n_free[("nw", t % 2)] = p.dma(Y, f"q_gnw{t % 2}",
                                                      self.kn2d[c0:c0 + TT], nslot(t))
                if part == 1:
                    for par in range(2):
                        p.waitt(G, ev_free[("qkw", par)])
                        p.waitt(G, n_free.get(("nw", par), ("pe", 0)))
                    p.wait(G, "cc")
                    p.inc(G.collective_compute(
                        "AllGather", OP.bypass, replica_groups=RG,
                        ins=[self.kn2d], outs=[self.kn2f]), "cc")
            p.wait(V, "dv")
            p.wait(G, "dv")
            qmw = p.gdma("g_qm", self.qm2d, qm2)
            p.waitt(G, qmw)
            ccq = p.inc(G.collective_compute("AllReduce", OP.max, replica_groups=RG,
                                             ins=[self.qm2d], outs=[self.qm2f]), "cc")
            p.wait(G, "cc", ccq)
            qmr = p.gdma("g_qm", qm2, self.qm2f)
            self._ga_ccq = qmr
            self._ga_cck = p.inc(G.collective_compute(
                "AllGather", OP.bypass, replica_groups=RG,
                ins=[self.kgin], outs=[self.kgf]), "cck")
            self.fence(sems=[nm for nm in p.cnt if nm != "cck"])
        with contextlib.ExitStack() as ph:
            ent = ph.enter_context
            wvT = ent(nc.sbuf_tensor(self.uname("ga_wv"), [128, 8 * 1024], BF16))
            h2t = [ent(nc.sbuf_tensor(self.uname("ga_h2"), [128, NCH * TT], BF16))
                   for i in range(2)]
            vst = ent(nc.sbuf_tensor(self.uname("ga_vst"), [128, 2048], BF16))
            bvr = ent(nc.sbuf_tensor(self.uname("ga_bvr"), [1, 1024], BF16))
            pt = self.ps[3][:, 512:768].bitcast(BF16)
            tr_ev = 0
            for fc in range(8):
                wl = p.gdma(f"g_gw{fc % 2}", vst[:, (fc % 2) * 1024:(fc % 2) * 1024 + 1024],
                            self.inp["attn_W"][2048 + fc * 128:2048 + fc * 128 + 128, :])
                p.waitt(TEN, wl)
                for dc in range(8):
                    p.wait(TEN, "ac", tr_ev)
                    mm = TEN.transpose(pt[:, 0:128],
                                       vst[:, (fc % 2) * 1024 + dc * 128:
                                           (fc % 2) * 1024 + dc * 128 + 128], self.ident)
                    pet = p.inc(mm, "pe")
                    p.wait(A, "pe", pet)
                    tr_ev = p.inc(A.activation(out=wvT[:, (dc * 8 + fc) * 128:
                                                       (dc * 8 + fc) * 128 + 128],
                                               in_=pt[:, 0:128], func=AF.Copy), "ac")
                p.wait(G, "pe", p.cnt["pe"])
            bvl = p.gdma("g_gbv", bvr[:, :], self.inp["attn_b"][0:1, 2048:3072])
            ev_free = {}
            h_free = {0: 0, 1: 0}
            p.wait(TEN, "ac", tr_ev)
            for t in range(NT):
                bi = t % 2
                c0 = t * TT
                hb = h2t[bi]
                p.wait(Y, "pe", h_free[bi])
                hl = p.dma(Y, f"q_gh{bi}", hb[:, :].rearrange("p (c t) -> p c t", c=NCH),
                           self.fm(self.h2d, c0))
                p.waitt(TEN, hl)
                for sub in range(4):
                    for half in range(2):
                        pv = self.bank(sub % 2 * 2 + half)
                        p.wait(TEN, "ac", ev_free.get(("v", sub % 2, half), 0))
                        for dc in range(8):
                            TEN.matmul(pv, hb[:, dc * TT + sub * 128: dc * TT + sub * 128 + 128],
                                       wvT[:, dc * 1024 + half * 512:dc * 1024 + half * 512 + 512],
                                       start=(dc == 0), stop=False)
                        p.waitt(TEN, bvl)
                        mm = TEN.matmul(pv, self.onesb[0:1, 0:128],
                                        bvr[:, half * 512:(half + 1) * 512],
                                        start=False, stop=True)
                        pev = p.inc(mm, "pe")
                        h_free[bi] = pev
                        p.wait(A, "pe", pev)
                        vs = vst[:, (sub % 2) * 1024 + half * 512:(sub % 2) * 1024 + half * 512 + 512]
                        p.waitt(A, ev_free.get(("vw", sub % 2), ("pe", 0)))
                        ev = p.inc(A.activation(out=vs, in_=pv, func=AF.Copy), "ac")
                        ev_free[("v", sub % 2, half)] = ev
                    p.wait(SC, "ac", ev)
                    jc = t * 4 + sub
                    ev_free[("vw", sub % 2)] = p.dma(
                        SC, f"q_gvw{sub % 2}", self.vgin[jc],
                        vst[:, (sub % 2) * 1024:(sub % 2) * 1024 + 1024])
            for par in range(2):
                p.waitt(G, ev_free[("vw", par)])
            p.wait(G, "cc")
            ccv = p.inc(G.collective_compute("AllGather", OP.bypass, replica_groups=RG,
                                             ins=[self.vgin], outs=[self.vgf]), "ccv")
            self._ga_ccv = ccv
            self.fence(sems=[nm for nm in p.cnt if nm != "cck"])
        with contextlib.ExitStack() as ph:
            ent = ph.enter_context
            kstg = [ent(nc.sbuf_tensor(self.uname("ga_kst"), [128, 1024], BF16)) for i in range(2)]
            ebuf = ent(nc.sbuf_tensor(self.uname("ga_eb"), [128, 4 * 512], BF16))
            est = [ent(nc.sbuf_tensor(self.uname("ga_es"), [128, 512], BF16)) for i in range(2)]
            obuf = ent(nc.sbuf_tensor(self.uname("ga_ob"), [128, NCH * TT], F32))
            ncol = ent(nc.sbuf_tensor(self.uname("ga_nc"), [128, 64 + 2], F32))
            zall = ent(nc.sbuf_tensor(self.uname("ga_z"), [128, 64 + 8], F32))
            st2 = ent(nc.sbuf_tensor(self.uname("ga_st2"), [1, 2], BF16))
            p.waitt(V, self._ga_ccq)
            qcol = ncol[:, 64:65]
            self.bcast_col(qm2, qcol, st2[0:1, 0:2])
            p.wait(G, "cc")
            with nc.allow_non_contiguous_dma(reason="one-time negC col load"):
                kl = p.gdma("g_kn2", ncol[:, 0:32],
                            self.kn2f[0].rearrange("(jc p) -> p jc", p=128))
                kl = p.gdma("g_kn2b", ncol[:, 32:64],
                            self.kn2f[1].rearrange("(jc p) -> p jc", p=128))
            p.waitt(V, kl)
            p.wait(V, "g_kn2", p.cnt["g_kn2"])
            p.wait(V, "ac")
            d1 = p.inc(V.tensor_scalar(out=ncol[:, 0:64], in0=ncol[:, 0:64], scalar1=qcol,
                                       scalar2=None, op0=OP.mult), "dv")
            p.wait(A, "dv", d1)
            a1 = p.inc(A.activation(out=ncol[:, 0:64], in_=ncol[:, 0:64], func=AF.Ln), "ac")
            p.wait(A, "ac", a1)
            a2 = p.inc(A.activation(out=ncol[:, 0:64], in_=ncol[:, 0:64], func=AF.Exp,
                                    scale=0.5, bias=-math.log(1024.0)), "ac")
            p.wait(V, "ac", a2)
            d2 = p.inc(V.tensor_scalar(out=ncol[:, 0:64], in0=ncol[:, 0:64], scalar1=-1.0,
                                       scalar2=None, op0=OP.mult), "dv")
            negc_dv = d2
            p.wait(Y, "cck", self._ga_cck)
            kld = {}
            e_free = {}
            z_free = 0
            ew = {}
            for jc in range(64):
                bi = jc % 2
                p.wait(Y, "pe", kld.get(("used", bi), 0))
                kl = p.dma(Y, f"q_gk{bi}", kstg[bi][:, :].rearrange("p (dc x) -> p dc x", dc=8),
                           self.kgf[jc // 32, jc % 32].rearrange("dc p x -> p dc x"))
                for it in range(NT):
                    ps = self.bank(it % 2)
                    p.waitt(TEN, kl)
                    p.wait(TEN, "ac", e_free.get(it % 2, 0))
                    for dc in range(8):
                        mm = TEN.matmul(ps, kstg[bi][:, dc * 128:(dc + 1) * 128],
                                        q_all(dc)[:, it * TT:(it + 1) * TT],
                                        start=(dc == 0), stop=(dc == 7))
                    peS = p.inc(mm, "pe")
                    kld[("used", bi)] = peS
                    p.wait(A, "pe", peS)
                    p.wait(A, "dv", negc_dv)
                    ei = (jc * NT + it) % 4
                    eb = ebuf[:, ei * 512:ei * 512 + 512]
                    p.waitt(A, ew.get(ei, ("pe", 0)))
                    p.wait(A, "dv", z_free)
                    ea = p.inc(A.activation(out=eb, in_=ps, func=AF.Exp, scale=1.0 / 1024.0,
                                            bias=ncol[:, jc:jc + 1],
                                            accum_out=zall[:, 64 + it:64 + it + 1]), "ac")
                    e_free[it % 2] = ea
                    p.wait(SC, "ac", ea)
                    ew[ei] = p.dma(SC, f"q_gew{ei}", self.ed[jc, it], eb)
                p.wait(V, "ac", e_free[0])
                p.wait(V, "ac", e_free[1])
                z_free = p.inc(V.tensor_reduce(out=zall[:, jc:jc + 1], in_=zall[:, 64:72],
                                               axis=mybir.AxisListType.X, op=OP.add), "dv")
            p.wait(Y, "dv", z_free)
            zw = p.dma(Y, "q_gzw", self.zd, zall[:, 0:64])
            p.waitt(G, zw)
            p.wait(G, "cc")
            ccz = p.inc(G.collective_compute("AllReduce", OP.add, replica_groups=RG,
                                             ins=[self.zd], outs=[self.ztd]), "cc")
            p.wait(G, "cc", ccz)
            zr = p.gdma("g_gzr", zall[:, 0:64], self.ztd)
            p.waitt(A, zr)
            a1 = p.inc(A.activation(out=zall[:, 0:64], in_=zall[:, 0:64], func=AF.Ln), "ac")
            p.wait(A, "ac", a1)
            a2 = p.inc(A.activation(out=zall[:, 0:64], in_=zall[:, 0:64], func=AF.Exp,
                                    scale=-1.0), "ac")
            p.wait(V, "ccv", self._ga_ccv)
            p.wait(G, "ccv", self._ga_ccv)
            vl_free = {}
            stage_dv = {}
            p.wait(G, "pe")
            for jc in range(64):
                bi = jc % 2
                p.waitt(G, vl_free.get(bi, ("pe", 0)))
                vl = p.gdma(f"g_gvr{bi}", kstg[bi][:, :], self.vgf[jc // 32, jc % 32])
                p.waitt(V, vl)
                p.wait(V, "ac", a2)
                d = p.inc(V.tensor_scalar(
                    out=vres(jc), in0=kstg[bi][:, :], scalar1=zall[:, jc:jc + 1],
                    scalar2=None, op0=OP.mult), "dv")
                vl_free[bi] = ("dv", d)
                stage_dv[jc] = d
            for ei in range(4):
                if ei in ew:
                    p.waitt(Y, ew[ei])
            el_used = {0: 0, 1: 0}
            ob_w = ("pe", 0)
            for itb in range(NT):
                for jc in range(64):
                    bi = jc % 2
                    p.wait(Y, "pe", el_used[bi])
                    el = p.dma(Y, f"q_ge{bi}", est[bi][:, :], self.ed[jc, itb])
                    p.waitt(TEN, el)
                    if jc == 0:
                        p.waitt(TEN, ob_w)
                        p.wait(TEN, "ac", a2)
                    if itb == 0:
                        p.wait(TEN, "dv", stage_dv[jc])
                    for dvc in range(8):
                        mm = TEN.matmul(self.bank(dvc), vres(jc)[:, dvc * 128:(dvc + 1) * 128],
                                        est[bi][:, :], start=(jc == 0), stop=(jc == 63))
                    el_used[bi] = p.inc(mm, "pe")
                pe2 = el_used[(64 - 1) % 2]
                p.wait(A, "pe", pe2)
                last = None
                for dvc in range(8):
                    last = A.activation(out=obuf[:, dvc * TT:(dvc + 1) * TT],
                                        in_=self.bank(dvc), func=AF.Copy)
                aev = p.inc(last, "ac")
                p.wait(Y, "ac", aev)
                ob_w = p.dma(Y, "q_gow", self.fm(self.aod, itb * TT),
                             obuf[:, :].rearrange("p (c t) -> p c t", c=NCH))
            self.fence()

    def mask(self, cqc, cr):
        i = cqc * 5 + cr
        return self.masksb[:, i * 128:(i + 1) * 128]

    def emit_layer_prep(self, l, st):
        nc, p = self.nc, self.p
        TEN, V, A, Y = nc.tensor, nc.vector, nc.scalar, nc.sync
        r = self.smalls
        qsr, ksr = r[0:1, 64:192], r[0:1, 192:320]
        sc = r[0:1, 320:326]
        p.wait(Y, "dv")
        p.dma(Y, "q_prep", qsr, self.inp["qscale"][l:l + 1, :])
        ld = p.dma(Y, "q_prep", ksr, self.inp["kscale"][l:l + 1, :])
        p.waitt(V, ld)
        p.wait(V, "dv")
        p.inc(V.reduce_max(out=sc[:, 0:1], in_=qsr, axis=mybir.AxisListType.X,
                           apply_absolute_value=True), "dv")
        p.wait(V, "dv")
        p.inc(V.reduce_max(out=sc[:, 1:2], in_=ksr, axis=mybir.AxisListType.X,
                           apply_absolute_value=True), "dv")
        p.wait(V, "dv")
        p.inc(V.tensor_tensor(out=sc[:, 2:3], in0=sc[:, 0:1], in1=sc[:, 1:2], op=OP.mult), "dv")
        p.wait(V, "dv")
        p.inc(V.memset(self.cbf[:, 258:260], 0.0), "dv")
        p.wait(V, "dv")
        p.inc(V.tensor_copy(out=self.cbf[0:1, 258:259], in_=sc[:, 2:3]), "dv")
        p.wait(TEN, "dv")
        p.waitt(TEN, self.consts_ld)
        p.wait(TEN, "ac")
        mmc = TEN.matmul(self.bank(4)[:, 0:2], self.onesb[0:1, 0:128],
                         self.cbf[0:1, 258:260], start=True, stop=True)
        mmq = TEN.matmul(self.ps[3][0:8, 0:128], self.onesrow[0:1, 0:8], qsr,
                         start=True, stop=True)
        mmk = TEN.matmul(self.ps[3][0:8, 512:640], self.onesrow[0:1, 0:8], ksr,
                         start=True, stop=True)
        peq = p.inc(mmk, "pe")
        p.wait(A, "pe", peq)
        p.inc(A.activation(out=self.negC, in_=self.bank(4)[:, 0:1], func=AF.Copy,
                           scale=-16.0), "ac")
        p.wait(V, "pe", peq)
        last = None
        for c in range(4):
            last = V.tensor_tensor(out=st["indqs"][0:8, c * 128:(c + 1) * 128],
                                   in0=self.ind8s[:, c * 128:(c + 1) * 128],
                                   in1=self.ps[3][0:8, 0:128], op=OP.mult)
            last = V.tensor_tensor(out=st["indks"][0:8, c * 128:(c + 1) * 128],
                                   in0=self.ind8s[:, c * 128:(c + 1) * 128],
                                   in1=self.ps[3][0:8, 512:640], op=OP.mult)
        p.inc(last, "dv")

    def tap_dram(self, name, src, shape, dtype):
        nc, p = self.nc, self.p
        d = nc.dram_tensor(name, shape, dtype, kind="ExternalOutput").ap()
        self.extra_outputs.append(name)
        self.fence(engines=[nc.sync])
        p.dma(nc.sync, "q_tap", d, src)
        return d

    def tap_fm(self, name, sbuf_src_fn):
        nc, p = self.nc, self.p
        d = nc.dram_tensor(name, [D, T], F32, kind="ExternalOutput").ap()
        self.extra_outputs.append(name)
        self.fence(engines=[nc.sync])
        p.dma(nc.sync, "q_tap", d.rearrange("(c p) t -> p c t", p=128), sbuf_src_fn())
        return d

    def build(self):
        nc = self.nc
        self.declare()
        with contextlib.ExitStack() as ctx:
            self.alloc(ctx)
            p = self.p
            with nc.Block() as block:
                @block.gpsimd
                def _(g):
                    self.load_consts()
                    self.emit_boundary(self.inp["x_fm"], add_src=None, write_x=None, make="y")
                    self.emit_ycol_exchange()
                    if "y1" in self.taps:
                        self.tap_fm("dbg_y1", lambda: self.y[:, :]
                                    .rearrange("p (c t) -> p c t", c=NCH)[:, :, 1:YST])
                    for l in range(2):
                        if self.upto < 2 + 3 * l:
                            break
                        self.emit_ffn_qkv(l)
                        if l == 0 and "qkv" in self.taps:
                            self.tap_dram("dbg_q", self.qd, [4, 128, T], BF16)
                            self.tap_dram("dbg_k", self.kd, [32, 4, 128, 128], BF16)
                            self.tap_dram("dbg_v", self.vd, [32, 128, INNER], BF16)
                            self.tap_dram("dbg_kh", self.khalo, [4, 4, 128, 128], BF16)
                            self.tap_dram("dbg_vh", self.vhalo, [4, 128, INNER], BF16)
                        if self.upto < 3 + 3 * l:
                            break
                        self.emit_ffn_attn(l)
                        if l == 0 and "yw" in self.taps:
                            self.tap_fm("dbg_yw", lambda: self.y[:, :]
                                        .rearrange("p (c t) -> p c t", c=NCH)[:, :, 1:YST])
                        if self.upto < 4 + 3 * l:
                            break
                        self.emit_ffn_mlp(l)
                        self.emit_ycol_exchange()
                        if l == 0 and "yl0" in self.taps:
                            self.tap_fm("dbg_yl0", lambda: self.y[:, :]
                                        .rearrange("p (c t) -> p c t", c=NCH)[:, :, 1:YST])
                    if "yc1" in self.taps:
                        self.tap_fm("dbg_yc1", lambda: self.y[:, :]
                                    .rearrange("p (c t) -> p c t", c=NCH)[:, :, 1:YST])
                    if self.upto >= 8:
                        self.emit_boundary(self.inp["x_fm"], add_src="y",
                                           write_x=self.x2, make="h2")
                    if self.upto >= 9:
                        self.emit_global_attn()
                        if "ga" in self.taps:
                            self.tap_dram("dbg_gao", self.aod, [D, T], F32)
                    if self.upto >= 10:
                        self.emit_boundary(self.x2, add_src=self.aod,
                                           write_x=self.x3, make="y")
                        self.emit_ycol_exchange()
                    for l in range(2):
                        if self.upto < 11 + 3 * l:
                            break
                        self.emit_ffn_qkv(l)
                        if self.upto < 12 + 3 * l:
                            break
                        self.emit_ffn_attn(l)
                        if self.upto < 13 + 3 * l:
                            break
                        self.emit_ffn_mlp(l)
                        self.emit_ycol_exchange()
                    if self.upto >= 17:
                        self.emit_boundary(self.x3, add_src="y",
                                           write_x=self.out_fm, make=None)
                    if self.upto < 17:
                        p.wait(nc.sync, "dv")
                        p.dma(nc.sync, "q_out", self.out_fm.rearrange("(c p) t -> p c t", p=128),
                              self.y[:, :].rearrange("p (c t) -> p c t", c=NCH)[:, :, 1:YST])
                    for nm, c in sorted(p.cnt.items()):
                        if nm.startswith(("q", "g_")) and c:
                            nc.sync.wait_ge(p.sems[nm], c)
        return nc


def build_nc(debug_taps=(), upto=99):
    mk = MK(debug_taps=debug_taps)
    mk.upto = upto
    return mk.build()


def kernel(**inputs):
    consts = [host_constants(c) for c in range(8)]
    nc = build_nc()
    in_maps = [make_in_map(inputs, c, consts) for c in range(8)]
    res = run_bass_kernel_spmd(nc, in_maps, core_ids=list(range(8)))
    out = np.empty((B, S, D), np.float32)
    for c, r in enumerate(res.results):
        b, h = c // 2, c % 2
        out[b, h * T:(h + 1) * T, :] = np.asarray(r["out_fm"]).T
    return out
